# version 13
# baseline (speedup 1.0000x reference)
import numpy as np
import ml_dtypes
import concourse.bass as bass
import concourse.mybir as mybir
from concourse.bass_utils import run_bass_kernel_spmd

F32 = mybir.dt.float32
BF16 = mybir.dt.bfloat16
ACT = mybir.ActivationFunctionType
ALU = mybir.AluOpType
AXX = mybir.AxisListType.X
BF = ml_dtypes.bfloat16

NCORE = 8
T = 2048
NT = 16
D = 1024
NH = 16
SEQ = 4096
RG = [[0, 1], [2, 3], [4, 5], [6, 7]]

DEBUG_STAGE = None

ENGS = ("sy", "gp", "ac", "ve", "mm")
SEMS = ENGS + ("wc", "cc", "d0", "d1", "s0", "s1", "s2")


class Plan:
    def __init__(self):
        self.q = {e: [] for e in ENGS}
        self.cnt = {s: 0 for s in SEMS}
        self.waited = {s: 0 for s in SEMS}
        self.selfw = {s: 0 for s in SEMS}

    def c(self, s):
        return self.cnt[s]

    def add(self, s, n=1):
        self.cnt[s] += n
        return self.cnt[s]

    def on(self, e, f):
        self.q[e].append(f)

    def wait(self, e, s, v=None):
        if v is None:
            v = self.cnt[s]
        if v <= 0:
            return
        if s == e and v > self.selfw[s]:
            self.selfw[s] = v
        self.waited[s] = max(self.waited[s], v)
        self.on(e, lambda eng, S, s=s, v=v: eng.wait_ge(S[s], v))

    def _pre_dma(self, e):
        w = self.waited[e]
        if w > self.selfw[e]:
            self.selfw[e] = w
            self.on(e, lambda eng, S, e=e, w=w: eng.wait_ge(S[e], w))

    def dma(self, e, out_ap, in_ap, sem=None):
        self._pre_dma(e)
        s = sem or e
        v = self.add(s, 16)
        self.on(e, lambda eng, S, o=out_ap, i=in_ap, s=s:
                eng.dma_start(out=o, in_=i).then_inc(S[s], 16))
        return v

    def dmat(self, e, out_ap, in_ap):
        self._pre_dma(e)
        v = self.add(e, 16)
        self.on(e, lambda eng, S, o=out_ap, i=in_ap, e=e:
                eng.dma_start_transpose(o, i).then_inc(S[e], 16))
        return v

    def op(self, e, f, sem=None):
        if e in ("ve", "ac") and self.cnt[e] > self.selfw[e]:
            self.wait(e, e)
        s = sem or e
        v = self.add(s, 1)
        self.on(e, lambda eng, S, f=f, s=s: f(eng).then_inc(S[s], 1))
        return v

    def opn(self, e, f):
        self.on(e, lambda eng, S, f=f: f(eng))

    def barrier(self):
        snap = dict(self.cnt)
        for e in ENGS:
            for s in SEMS:
                if snap[s] > 0 and (s != e or e in ("sy", "gp")):
                    self.wait(e, s, snap[s])


def _ap(x):
    return x if isinstance(x, bass.AP) else x[:]


def _mmf(ps, lhsT, rhs, start, stop, tp=None):
    ps, lhsT, rhs = _ap(ps), _ap(lhsT), _ap(rhs)

    def f(eng):
        return eng.matmul(ps, lhsT, rhs, start=start, stop=stop,
                          tile_position=tp, skip_group_check=True)
    return f



def _pack_lhsT(w, ocs=128):
    O, Din = w.shape
    noc, ndc = O // ocs, Din // 128
    r = w.reshape(noc, ocs, ndc, 128).transpose(0, 3, 2, 1)
    return np.ascontiguousarray(r.reshape(noc, 128, ndc * ocs))


def _pack_rhs(w):
    Din, O = w.shape
    return np.ascontiguousarray(w.reshape(Din // 128, 128, O))


def _pp(vec, ncols):
    return np.ascontiguousarray(np.asarray(vec).reshape(ncols, 128).T)


def make_in_maps(inp):
    f32 = lambda a: np.ascontiguousarray(np.asarray(a, dtype=np.float32))
    x_full = f32(inp["L0_attn_x"])
    wqkv0 = f32(inp["L0_attn_wqkv"])
    perm = np.concatenate([np.arange(192 * h + 64 * j, 192 * h + 64 * (j + 1))
                           for j in range(3) for h in range(NH)])
    w1qkv = f32(inp["L1_attn_w_qkv"])
    shared = {
        "w0qkv": _pack_rhs(np.ascontiguousarray(wqkv0[:, perm])),
        "w0w1": _pack_lhsT(f32(inp["L0_ffn_w1_w"])),
        "w0w2": _pack_lhsT(f32(inp["L0_ffn_w2_w"])),
        "w0h1": _pack_lhsT(f32(inp["L0_ffn_h1_w"])),
        "w0h2": _pack_lhsT(f32(inp["L0_ffn_h2_w"])),
        "w0w3": _pack_lhsT(f32(inp["L0_ffn_w3_w"])),
        "s1pp": _pp(f32(inp["L0_ffn_w1_s"]), 32),
        "s2pp": _pp(f32(inp["L0_ffn_w2_s"]), 32),
        "hs1pp": _pp(f32(inp["L0_ffn_h1_s"]), 32),
        "hs2pp": _pp(f32(inp["L0_ffn_h2_s"]), 32),
        "s3pp": _pp(f32(inp["L0_ffn_w3_s"]), 8),
        "w1q": _pack_lhsT(w1qkv[0:1024]),
        "w1k": _pack_lhsT(w1qkv[1024:2048]),
        "w1v": _pack_rhs(np.ascontiguousarray(w1qkv[2048:3072].T)),
        "bqpp": _pp(f32(inp["L1_attn_b_qkv"])[0:1024], 8),
        "bkpp": _pp(f32(inp["L1_attn_b_qkv"])[1024:2048], 8),
        "bvex": np.ascontiguousarray(np.broadcast_to(
            f32(inp["L1_attn_b_qkv"])[2048:3072], (128, D))),
        "w1o": _pack_rhs(np.ascontiguousarray(f32(inp["L1_attn_w_o"]).T)),
        "boex": np.ascontiguousarray(np.broadcast_to(
            f32(inp["L1_attn_b_o"]), (128, D))),
        "cn1": _pack_lhsT(f32(inp["L1_ffn_cn_w1"]).T),
        "cn2": _pack_lhsT(f32(inp["L1_ffn_cn_w2"]).T),
        "cn3": _pack_lhsT(f32(inp["L1_ffn_cn_w3"]).T),
        "cn4": _pack_rhs(f32(inp["L1_ffn_cn_w4"])),
        "b1pp": _pp(f32(inp["L1_ffn_cn_b1"]), 8),
        "b2pp": _pp(f32(inp["L1_ffn_cn_b2"]), 4),
        "b3pp": _pp(f32(inp["L1_ffn_cn_b3"]), 2),
        "b4ex": np.ascontiguousarray(np.broadcast_to(
            f32(inp["L1_ffn_cn_b4"]), (128, 1))),
        "gc1": _pack_rhs(f32(inp["L1_ffn_gc_w1"])),
        "gc2": _pack_rhs(f32(inp["L1_ffn_gc_w2"])),
        "gc3": _pack_rhs(f32(inp["L1_ffn_gc_w3"])),
        "gb1": f32(inp["L1_ffn_gc_b1"]).reshape(1, 512),
        "gb2": f32(inp["L1_ffn_gc_b2"]).reshape(1, 256),
        "gb3": f32(inp["L1_ffn_gc_b3"]).reshape(1, 1),
        "arex": np.ascontiguousarray(np.broadcast_to(
            f32(inp["L1_ffn_adaptive_range"]), (128, 1))),
    }
    maps = []
    for c in range(NCORE):
        b, h = c // 2, c % 2
        gidx = np.arange(T, dtype=np.float64) + h * T
        mx = np.maximum(gidx, 1.0)
        neg_lni = -np.log(mx)
        neg_lni[gidx < 1] = 40.0
        m = dict(shared)
        m["x0"] = np.ascontiguousarray(x_full[b, h * T:(h + 1) * T, :])
        m["neglni"] = _pp(neg_lni.astype(np.float32), 16)
        m["invtm"] = _pp((1.0 / mx).astype(np.float32), 16)
        m["cflag"] = np.full((1, 1), float(h), dtype=np.float32)
        maps.append(m)
    return maps



_NC_CACHE = {}


def build_nc(stage):
    nc = bass.Bass("TRN2", target_bir_lowering=False)
    pl = Plan()

    DI = {}
    for name, shape in (
        ("x0", (T, D)), ("w0qkv", (8, 128, 3072)), ("w0w1", (32, 128, 1024)),
        ("w0w2", (32, 128, 1024)), ("w0h1", (32, 128, 4096)),
        ("w0h2", (32, 128, 4096)), ("w0w3", (8, 128, 4096)),
        ("s1pp", (128, 32)), ("s2pp", (128, 32)), ("hs1pp", (128, 32)),
        ("hs2pp", (128, 32)), ("s3pp", (128, 8)),
        ("w1q", (8, 128, 1024)), ("w1k", (8, 128, 1024)),
        ("w1v", (8, 128, 1024)), ("bqpp", (128, 8)), ("bkpp", (128, 8)),
        ("bvex", (128, D)), ("w1o", (8, 128, 1024)), ("boex", (128, D)),
        ("cn1", (8, 128, 1024)), ("cn2", (4, 128, 1024)),
        ("cn3", (2, 128, 512)), ("cn4", (2, 128, 1)),
        ("b1pp", (128, 8)), ("b2pp", (128, 4)), ("b3pp", (128, 2)),
        ("b4ex", (128, 1)), ("gc1", (8, 128, 512)), ("gc2", (4, 128, 256)),
        ("gc3", (2, 128, 1)), ("gb1", (1, 512)), ("gb2", (1, 256)),
        ("gb3", (1, 1)), ("arex", (128, 1)), ("neglni", (128, 16)),
        ("invtm", (128, 16)), ("cflag", (1, 1)),
    ):
        DI[name] = nc.dram_tensor(name, list(shape), F32,
                                  kind="ExternalInput").ap()

    yout = nc.dram_tensor("y", [T, D], F32, kind="ExternalOutput").ap()

    cc_a_in = nc.dram_tensor("cc_a_in", [1, D], F32, kind="Internal").ap()
    cc_a_out = nc.dram_tensor("cc_a_out", [2, D], F32, kind="Internal").ap()
    cc_g_in = nc.dram_tensor("cc_g_in", [128, 8], F32, kind="Internal").ap()
    cc_g_out = nc.dram_tensor("cc_g_out", [256, 8], F32, kind="Internal").ap()
    cc_k_in = nc.dram_tensor("cc_k_in", [D, T], BF16, kind="Internal").ap()
    cc_k_out = nc.dram_tensor("cc_k_out", [2 * D, T], BF16, kind="Internal").ap()
    ccv_in = nc.dram_tensor("ccv_in", [T, D], BF16, kind="Internal").ap()
    ccv_out = nc.dram_tensor("ccv_out", [2 * T, D], BF16, kind="Internal").ap()
    cc_n_in = nc.dram_tensor("cc_n_in", [1, NH], F32, kind="Internal").ap()
    cc_n_out = nc.dram_tensor("cc_n_out", [2, NH], F32, kind="Internal").ap()

    SC = {}
    for nm, shp in (("w0w1", (32, 128, 1024)), ("w0w2", (32, 128, 1024)),
                    ("w0h1", (32, 128, 4096)), ("w0h2", (32, 128, 4096)),
                    ("w0w3", (8, 128, 4096))):
        SC[nm] = nc.dram_tensor("sc_" + nm, list(shp), BF16,
                                kind="Internal").ap()

    identf_d = nc.inline_tensor(np.eye(128, dtype=np.float32), "identf_d").ap()
    lstrict_d = nc.inline_tensor(
        (np.arange(128)[:, None] < np.arange(128)[None, :]).astype(BF),
        "lstrict_d").ap()
    aux_np = np.zeros((128, 8), dtype=BF)
    aux_np[:, 0] = 1
    aux_np[0:64, 1] = 1
    aux_np[64:128, 2] = 1
    aux_d = nc.inline_tensor(aux_np, "aux_d").ap()
    crow_d = nc.inline_tensor(np.ones((1, 128), dtype=BF), "crow_d").ap()
    ones2_d = nc.inline_tensor(np.ones((128, 2), dtype=BF), "ones2_d").ap()
    sel2_np = np.zeros((2, 128), dtype=BF)
    sel2_np[0, 0:64] = 1
    sel2_np[1, 64:128] = 1
    sel2_d = nc.inline_tensor(sel2_np, "sel2_d").ap()
    selb_d = nc.inline_tensor(np.ascontiguousarray(sel2_np[1:2]), "selb_d").ap()

    base = (nc.sbuf_base + 63) // 64 * 64
    cur = [base]

    def sb(name, shape, dtype):
        nbytes = int(np.prod(shape[1:])) * mybir.dt.size(dtype)
        at = cur[0]
        cur[0] = at + (nbytes + 31) // 32 * 32
        return nc.alloc_sbuf_tensor_at(name, list(shape), dtype, offset=at)

    RESID = sb("RESID", [128, NT * D], F32)
    XN = sb("XN", [128, 16384], BF16)
    RX = sb("RX", [128, 16384], BF16)
    RY = sb("RY", [128, 16384], BF16)
    SLAB = sb("SLAB", [128, 8192], BF16)
    LAND = sb("LAND", [128, 3584], F32)
    SCRF = sb("SCRF", [128, 2048], F32)
    SCRB = sb("SCRB", [128, 2048], BF16)

    identf = sb("identf", [128, 128], F32)
    lstrict = sb("lstrict", [128, 128], BF16)
    auxsb = sb("auxsb", [128, 8], BF16)
    crow = sb("crow", [1, 128], BF16)
    ones2 = sb("ones2", [128, 2], BF16)
    sel2sb = sb("sel2sb", [2, 128], BF16)
    slni = sb("slni", [128, 16], F32)
    sinv = sb("sinv", [128, 16], F32)
    s1c = sb("s1c", [128, 32], F32)
    s2c = sb("s2c", [128, 32], F32)
    hs1c = sb("hs1c", [128, 32], F32)
    hs2c = sb("hs2c", [128, 32], F32)
    s3c = sb("s3c", [128, 8], F32)
    bqc = sb("bqc", [128, 8], F32)
    bkc = sb("bkc", [128, 8], F32)
    b1c = sb("b1c", [128, 8], F32)
    b2c = sb("b2c", [128, 4], F32)
    b3c = sb("b3c", [128, 2], F32)
    b4c = sb("b4c", [128, 1], F32)
    arc = sb("arc", [128, 1], F32)
    flagc = sb("flagc", [1, 1], F32)
    rcol = sb("rcol", [128, 16], F32)
    mrcol = sb("mrcol", [128, 16], F32)
    stile = sb("stile", [128, 256], F32)
    gfcolb = sb("gfcolb", [128, 8], BF16)
    gfc = sb("gfc", [128, 1], F32)
    acol = sb("acol", [128, 1], F32)
    bcol = sb("bcol", [128, 1], F32)
    ccols = sb("ccols", [128, 2], F32)
    ncols_ = sb("ncols_", [128, 2], F32)
    qkn = sb("qkn", [1, 64], F32)
    selb = sb("selb", [1, 128], BF16)
    negmb = sb("negmb", [1, NH], BF16)
    negm = sb("negm", [128, NH], F32)
    cn4b = sb("cn4b", [128, 2], BF16)
    gfb = sb("gfb", [1, 2], BF16)
    sigt = sb("sigt", [128, 16], F32)
    w2t = sb("w2t", [128, 16], F32)
    assert cur[0] <= nc.sbuf_top, (cur[0], nc.sbuf_top)

    tfcol = sigt[:, 0:1]
    gfcol = negm[:, 0:8]

    scr_silu0 = SCRF[:, 0:512]
    scr_silu1 = SCRF[:, 512:1024]
    scr_pp = SCRF[:, 1024:2048]
    scr_cast = SCRB[:, 0:1024]
    scr_sq = SCRB[:, 1024:2048]
    lnsq = LAND[:, 0:1024]
    tots = LAND[0:1, 0:1024]
    carry = LAND[0:1, 1024:2048]
    carryscr = LAND[0:1, 2048:3072]
    carrybf = SLAB[0:1, 0:2048]
    gfrow = SCRF[0:1, 0:512]
    gfrow2 = SCRF[0:1, 512:1024]
    zrowA = SCRF[0:1, 0:512]
    zrowB = SCRF[0:1, 512:1024]
    zinvA = SCRF[0:1, 1024:1536]
    zinvB = SCRF[0:1, 1536:2048]
    zbf = XN[0:1, 14336:16384]
    nrow = qkn[0:1, 16:32]
    qrow = qkn[0:1, 0:16]

    PS = [nc.alloc_psum_tensor(f"psb{i}", [128, 512], F32) for i in range(8)]

    _wbv = {}

    def stream_sets(sets, consume, land_base=0, land_slots=3, land_cols=1024,
                    pre_guard=None, wb=None):
        mm_entry = pl.c("mm")
        wc_entry = pl.c("wc")
        g = [0]
        castv = []
        mmv = {}
        wbv = {}

        def load_set(si):
            mm_guard = mmv.get(si - 2, mm_entry if pre_guard is None
                               else pre_guard)
            for j, (src, ncols, dst) in enumerate(sets[si]):
                slot = g[0] % land_slots
                qq = ("sy", "gp")[g[0] % 2]
                pl.wait(qq, "wc", max(wc_entry,
                                      pl.c("wc") - land_slots + 1))
                la = LAND[:, land_base + slot * land_cols:
                          land_base + slot * land_cols + ncols]
                dv = pl.dma(qq, la, src, sem="s%d" % slot)
                pl.wait("ve", "s%d" % slot, dv)
                if j == 0 and mm_guard > 0:
                    pl.wait("ve", "mm", mm_guard)
                if wb is not None and (si - 2, j) in wbv:
                    pl.wait("ve", "sy", wbv[(si - 2, j)])
                cv = pl.op("ve", lambda e, la=la, dst=dst:
                           e.tensor_copy(dst, la), sem="wc")
                if wb is not None:
                    pl.wait("sy", "wc", cv)
                    wbv[(si, j)] = pl.dma("sy", wb[si][j], dst)
                g[0] += 1

        nsets = len(sets)
        for si in range(min(2, nsets)):
            load_set(si)
            castv.append(pl.c("wc"))
        for si in range(nsets):
            pl.wait("mm", "wc", castv[si])
            consume(si)
            mmv[si] = pl.c("mm")
            if si + 2 < nsets:
                load_set(si + 2)
                castv.append(pl.c("wc"))

    def stream_sets_direct(sets, consume):
        mm_entry = pl.c("mm")
        mmv = {}
        gv = {}

        def load_set(si):
            guard = mmv.get(si - 2, mm_entry)
            last = {"sy": 0, "gp": 0}
            seen = set()
            for j, (src, ncols, dst) in enumerate(sets[si]):
                qq = ("sy", "gp")[(si + j) % 2]
                if qq not in seen and guard > 0:
                    pl.wait(qq, "mm", guard)
                    seen.add(qq)
                last[qq] = pl.dma(qq, dst, src)
            gv[si] = last

        nsets = len(sets)
        for si in range(min(2, nsets)):
            load_set(si)
        for si in range(nsets):
            if gv[si]["sy"]:
                pl.wait("mm", "sy", gv[si]["sy"])
            if gv[si]["gp"]:
                pl.wait("mm", "gp", gv[si]["gp"])
            consume(si)
            mmv[si] = pl.c("mm")
            if si + 2 < nsets:
                load_set(si + 2)

    def slabdst(si, off, ncols):
        b = (si % 2) * 4096
        return SLAB[:, b + off:b + off + ncols]

    for t_, d_ in ((identf, identf_d), (lstrict, lstrict_d),
                   (auxsb, aux_d), (crow, crow_d), (ones2, ones2_d),
                   (sel2sb, sel2_d),
                   (selb, selb_d),
                   (slni, DI["neglni"]), (sinv, DI["invtm"]),
                   (s1c, DI["s1pp"]), (s2c, DI["s2pp"]), (hs1c, DI["hs1pp"]),
                   (hs2c, DI["hs2pp"]), (s3c, DI["s3pp"]), (bqc, DI["bqpp"]),
                   (bkc, DI["bkpp"]), (b1c, DI["b1pp"]), (b2c, DI["b2pp"]),
                   (b3c, DI["b3pp"]), (b4c, DI["b4ex"]), (arc, DI["arex"]),
                   (flagc, DI["cflag"])):
        pl.dma("gp", t_[:], d_)
    for tile in range(NT):
        pl.dma("sy", RESID[:, tile * D:(tile + 1) * D],
               DI["x0"][tile * 128:(tile + 1) * 128, :])
    pl.barrier()

    ST = 1536

    def plan_ln(tile):
        rt = RESID[:, tile * D:(tile + 1) * D]
        pl.op("ve", lambda e, rt=rt: e.tensor_tensor(lnsq, rt, rt, ALU.mult))
        pl.op("ve", lambda e: e.reduce_sum(SCRF[:, ST:ST + 1], lnsq, axis=AXX))
        pl.op("ve", lambda e, rt=rt: e.reduce_sum(SCRF[:, ST + 1:ST + 2], rt,
                                                  axis=AXX))
        pl.op("ve", lambda e: e.tensor_scalar(
            SCRF[:, ST + 2:ST + 3], SCRF[:, ST + 1:ST + 2], 1.0 / D, None,
            ALU.mult))
        pl.op("ve", lambda e: e.tensor_tensor(
            SCRF[:, ST + 3:ST + 4], SCRF[:, ST + 2:ST + 3],
            SCRF[:, ST + 2:ST + 3], ALU.mult))
        pl.op("ve", lambda e: e.tensor_scalar(
            SCRF[:, ST + 4:ST + 5], SCRF[:, ST:ST + 1], 1.0 / D, 1e-5,
            ALU.mult, ALU.add))
        v = pl.op("ve", lambda e: e.tensor_tensor(
            SCRF[:, ST + 4:ST + 5], SCRF[:, ST + 4:ST + 5],
            SCRF[:, ST + 3:ST + 4], ALU.subtract))
        pl.wait("ac", "ve", v)
        v = pl.op("ac", lambda e: e.activation(
            SCRF[:, ST + 5:ST + 6], SCRF[:, ST + 4:ST + 5], ACT.Sqrt))
        pl.wait("ve", "ac", v)
        pl.op("ve", lambda e, tile=tile: e.reciprocal(
            rcol[:, tile:tile + 1], SCRF[:, ST + 5:ST + 6]))
        pl.op("ve", lambda e, tile=tile: e.tensor_tensor(
            mrcol[:, tile:tile + 1], SCRF[:, ST + 2:ST + 3],
            rcol[:, tile:tile + 1], ALU.mult))
        vx = pl.op("ve", lambda e, rt=rt, tile=tile: e.tensor_scalar(
            scr_pp, rt, rcol[:, tile:tile + 1], mrcol[:, tile:tile + 1],
            ALU.mult, ALU.subtract))
        pl.wait("mm", "ve", vx)
        evd = {}
        tvl = 0
        for c in range(8):
            bank = PS[6 + (c % 2)]
            if c >= 2:
                pl.wait("mm", "ac", evd[c - 2])
            tvl = pl.op("mm", lambda e, c=c, bank=bank: e.transpose(
                bank[:, 0:128], scr_pp[:, c * 128:c * 128 + 128], identf[:]))
            pl.wait("ac", "mm", tvl)
            evd[c] = pl.op("ac", lambda e, c=c, tile=tile, bank=bank:
                           e.activation(XN[:, c * 2048 + tile * 128:
                                           c * 2048 + tile * 128 + 128],
                                        bank[:, 0:128], ACT.Identity))
        pl.wait("ve", "mm", tvl)

    for tile in range(NT):
        plan_ln(tile)
    pl.barrier()

    if stage == "x0":
        return _finish(nc, pl, RESID, yout)

    qkv0_sets = []
    for ccn in range(6):
        qkv0_sets.append([(DI["w0qkv"][dc, :, ccn * 512:(ccn + 1) * 512], 512,
                           slabdst(ccn, dc * 512, 512)) for dc in range(8)])

    _q0evac = {}

    def qkv0_consume(ccn):
        half = ccn % 2
        for tile in range(NT):
            key = ccn * NT + tile
            bank = PS[key % 4]
            if key >= 4:
                sname, sval = _q0evac[key - 4]
                pl.wait("mm", sname, sval)
            mv = None
            for dc in range(8):
                f = _mmf(bank, XN[:, dc * 2048 + tile * 128:
                                  dc * 2048 + tile * 128 + 128],
                         slabdst(ccn, dc * 512, 512), dc == 0, dc == 7)
                if dc == 7:
                    mv = pl.op("mm", f)
                else:
                    pl.opn("mm", f)
            sl = slice(tile * D + half * 512, tile * D + half * 512 + 512)
            if ccn in (0, 1):
                pl.wait("ac", "mm", mv)
                ev = pl.op("ac", lambda e, bank=bank, sl=sl:
                           e.copy(RY[:, sl], bank[:]))
                _q0evac[key] = ("ac", ev)
            elif ccn in (2, 3):
                pl.wait("ve", "mm", mv)
                pl.wait("ve", "ac", _q0evac[(ccn - 2) * NT + tile][1])
                pl.op("ve", lambda e, bank=bank, sl=sl: e.tensor_tensor(
                    scr_silu0, RY[:, sl], bank[:], ALU.mult))
                v2 = None
                for hh in range(8):
                    h = (ccn - 2) * 8 + hh
                    v2 = pl.op("ve", lambda e, hh=hh, h=h, tile=tile:
                               e.reduce_sum(
                                   stile[:, tile * 16 + h:tile * 16 + h + 1],
                                   scr_silu0[:, hh * 64:(hh + 1) * 64],
                                   axis=AXX))
                _q0evac[key] = ("ve", v2)
            else:
                pl.wait("ac", "mm", mv)
                ev = pl.op("ac", lambda e, bank=bank, sl=sl:
                           e.copy(RX[:, sl], bank[:]))
                _q0evac[key] = ("ac", ev)
                pl.wait("mm", "ac", ev)
                tb = PS[6] if half == 0 else PS[7]
                f = _mmf(tb[0:1, :], auxsb[:, 0:1], RX[:, sl],
                         tile == 0, tile == NT - 1)
                if tile == NT - 1:
                    pl.op("mm", f)
                else:
                    pl.opn("mm", f)

    stream_sets(qkv0_sets, qkv0_consume)
    pl.barrier()

    pl.op("ve", lambda e: e.tensor_copy(tots[0:1, 0:512], PS[6][0:1, :]))
    v = pl.op("ve", lambda e: e.tensor_copy(tots[0:1, 512:1024], PS[7][0:1, :]))
    pl.wait("gp", "ve", v)
    gv = pl.dma("gp", cc_a_in, tots[:])
    pl.wait("gp", "gp", gv)
    ccv = pl.op("gp", lambda e: e.collective_compute(
        "AllGather", ALU.bypass, replica_groups=RG,
        ins=[cc_a_in.opt()], outs=[cc_a_out.opt()]), sem="cc")
    pl.wait("gp", "cc", ccv)
    gv = pl.dma("gp", scr_pp[0:1, :], cc_a_out[0:1, :])
    pl.wait("ve", "gp", gv)
    pl.op("ve", lambda e: e.tensor_scalar(
        carry[:], scr_pp[0:1, :], flagc[0:1, 0:1], None, ALU.mult))

    def carry_hilo():
        pl.op("ve", lambda e: e.tensor_copy(carrybf[0:1, 0:1024], carry[:]))
        pl.op("ve", lambda e: e.tensor_tensor(
            carryscr, carry[:], carrybf[0:1, 0:1024], ALU.subtract))
        return pl.op("ve", lambda e: e.tensor_copy(carrybf[0:1, 1024:2048],
                                                   carryscr))

    cv = carry_hilo()
    pl.barrier()

    for tile in range(NT):
        pl.wait("ac", "ve")
        av = pl.op("ac", lambda e, tile=tile: e.activation(
            sigt[:], stile[:, tile * 16:tile * 16 + 16], ACT.Sigmoid,
            bias=slni[:, tile:tile + 1], scale=0.125))
        pl.wait("ve", "ac", av)
        pl.op("ve", lambda e: e.tensor_scalar(w2t[:], sigt[:], -1.0, 1.0,
                                              ALU.mult, ALU.add))
        wsig = pl.op("ve", lambda e, tile=tile: e.tensor_scalar(
            w2t[:], w2t[:], sinv[:, tile:tile + 1], None, ALU.mult))
        for half in range(2):
            bank = PS[half]
            sl = slice(tile * D + half * 512, tile * D + half * 512 + 512)
            pl.opn("mm", _mmf(bank, lstrict[:], RX[:, sl], True, False))
            pl.opn("mm", _mmf(bank, crow[:],
                              carrybf[0:1, half * 512:half * 512 + 512],
                              False, False))
            mv = pl.op("mm", _mmf(bank, crow[:],
                                  carrybf[0:1, 1024 + half * 512:
                                          1024 + half * 512 + 512],
                                  False, True))
            pl.wait("ve", "mm", mv)
            pl.wait("ve", "ve", wsig)
            sig_e = sigt[:, half * 8:half * 8 + 8].unsqueeze(2) \
                .broadcast_to([128, 8, 64])
            w2_e = w2t[:, half * 8:half * 8 + 8].unsqueeze(2) \
                .broadcast_to([128, 8, 64])
            pl.op("ve", lambda e, sl=sl, sig_e=sig_e: e.tensor_tensor(
                scr_silu0.rearrange("p (a b) -> p a b", a=8),
                RX[:, sl].rearrange("p (a b) -> p a b", a=8), sig_e, ALU.mult))
            pl.op("ve", lambda e, bank=bank, w2_e=w2_e: e.tensor_tensor(
                scr_silu1.rearrange("p (a b) -> p a b", a=8),
                bank[:].rearrange("p (a b) -> p a b", a=8), w2_e, ALU.mult))
            pl.op("ve", lambda e, sl=sl: e.tensor_tensor(
                RESID[:, sl], RESID[:, sl], scr_silu0, ALU.add))
            v3 = pl.op("ve", lambda e, sl=sl: e.tensor_tensor(
                RESID[:, sl], RESID[:, sl], scr_silu1, ALU.add))
        if tile < NT - 1:
            for half in range(2):
                tb = PS[2 + half]
                sl = slice(tile * D + half * 512, tile * D + half * 512 + 512)
                mv = pl.op("mm", _mmf(tb[0:1, :], auxsb[:, 0:1], RX[:, sl],
                                      True, True))
                pl.wait("ve", "mm", mv)
                pl.op("ve", lambda e, tb=tb, half=half: e.tensor_tensor(
                    carry[0:1, half * 512:half * 512 + 512],
                    carry[0:1, half * 512:half * 512 + 512], tb[0:1, :],
                    ALU.add))
            cv = carry_hilo()
            pl.wait("mm", "ve", cv)
    pl.barrier()

    if stage == "attn0":
        return _finish(nc, pl, RESID, yout)

    def ffn0(tt):
        first = tt == 0
        if not first:
            pl.wait("gp", "sy", _wb_done[0])
        pl.wait("ve", "mm")
        pl.wait("ac", "mm")
        s1_sets = []
        s1_wb = []
        for oc in range(32):
            for wname in ("w0w1", "w0w2"):
                si = len(s1_sets)
                src = DI[wname] if first else SC[wname]
                s1_sets.append([(src[oc, :, 0:1024], 1024,
                                 slabdst(si, 0, 1024))])
                s1_wb.append([SC[wname][oc, :, 0:1024]])

        _s1evac = {}

        def s1_consume(si):
            oc, w = si // 2, si % 2
            bank = PS[w * 2 + (oc % 2)]
            if w == 0 and oc >= 2:
                pl.wait("mm", "ve", _s1evac[oc - 2])
            mv = None
            for dc in range(8):
                f = _mmf(bank, slabdst(si, dc * 128, 128),
                         XN[:, dc * 2048 + tt * 512:dc * 2048 + tt * 512 + 512],
                         dc == 0, dc == 7)
                if dc == 7:
                    mv = pl.op("mm", f)
                else:
                    pl.opn("mm", f)
            if w == 1:
                b1, b2 = PS[(oc % 2)], PS[2 + (oc % 2)]
                slt = scr_silu0 if oc % 2 == 0 else scr_silu1
                pl.wait("ac", "mm", mv)
                if oc >= 2:
                    pl.wait("ac", "ve", _s1evac[oc - 2])
                av = pl.op("ac", lambda e, b1=b1, oc=oc, slt=slt: e.activation(
                    slt, b1[:], ACT.Silu, scale=s1c[:, oc:oc + 1]))
                pl.wait("ve", "ac", av)
                vv = pl.op("ve", lambda e, b2=b2, oc=oc, slt=slt:
                           e.scalar_tensor_tensor(
                               RX[:, oc * 512:oc * 512 + 512], b2[:],
                               s2c[:, oc:oc + 1], slt, ALU.mult, ALU.mult))
                _s1evac[oc] = vv

        if first:
            stream_sets(s1_sets, s1_consume, wb=s1_wb)
        else:
            stream_sets_direct(s1_sets, s1_consume)
        pl.wait("mm", "ve")

        def hstage(wname, srcbuf, dstbuf, scol):
            src = DI[wname] if first else SC[wname]
            sets = []
            wbl = []
            for oc in range(32):
                sets.append([(src[oc, :, gpc * 1024:(gpc + 1) * 1024],
                              1024, slabdst(oc, gpc * 1024, 1024))
                             for gpc in range(4)])
                wbl.append([SC[wname][oc, :, gpc * 1024:(gpc + 1) * 1024]
                            for gpc in range(4)])
            _hevac = {}

            def consume(oc):
                bank = PS[oc % 4]
                if oc >= 4:
                    pl.wait("mm", "ac", _hevac[oc - 4])
                mv = None
                for dc in range(32):
                    f = _mmf(bank, slabdst(oc, dc * 128, 128),
                             srcbuf[:, dc * 512:dc * 512 + 512],
                             dc == 0, dc == 31)
                    if dc == 31:
                        mv = pl.op("mm", f)
                    else:
                        pl.opn("mm", f)
                pl.wait("ac", "mm", mv)
                av = pl.op("ac", lambda e, bank=bank, oc=oc: e.activation(
                    dstbuf[:, oc * 512:oc * 512 + 512], bank[:], ACT.Gelu,
                    scale=scol[:, oc:oc + 1]))
                _hevac[oc] = av

            if first:
                stream_sets(sets, consume, wb=wbl)
            else:
                stream_sets_direct(sets, consume)
            pl.wait("mm", "ac")

        hstage("w0h1", RX, RY, hs1c)
        hstage("w0h2", RY, RX, hs2c)

        w3src = DI["w0w3"] if first else SC["w0w3"]
        w3_sets = []
        w3_wb = []
        for ec in range(8):
            w3_sets.append([(w3src[ec, :, gpc * 1024:(gpc + 1) * 1024],
                             1024, slabdst(ec, gpc * 1024, 1024))
                            for gpc in range(4)])
            w3_wb.append([SC["w0w3"][ec, :, gpc * 1024:(gpc + 1) * 1024]
                          for gpc in range(4)])

        def w3_consume(ec):
            bank = PS[4 + ec % 2]
            mv = None
            for dc in range(32):
                f = _mmf(bank, slabdst(ec, dc * 128, 128),
                         RX[:, dc * 512:dc * 512 + 512], dc == 0, dc == 31)
                if dc == 31:
                    mv = pl.op("mm", f)
                else:
                    pl.opn("mm", f)
            pl.wait("ac", "mm", mv)
            if ec >= 1:
                pl.wait("ac", "mm")
            av = pl.op("ac", lambda e, bank=bank, ec=ec: e.activation(
                scr_silu1, bank[:], ACT.Identity, scale=s3c[:, ec:ec + 1]))
            pl.wait("mm", "ac", av)
            for ts_ in range(4):
                tv = pl.op("mm", lambda e, ts_=ts_: e.transpose(
                    PS[6][:, 0:128], scr_silu1[:, ts_ * 128:ts_ * 128 + 128],
                    identf[:]))
                pl.wait("ve", "mm", tv)
                gt = tt * 4 + ts_
                vv = pl.op("ve", lambda e, gt=gt, ec=ec: e.tensor_tensor(
                    RESID[:, gt * D + ec * 128:gt * D + ec * 128 + 128],
                    RESID[:, gt * D + ec * 128:gt * D + ec * 128 + 128],
                    PS[6][:, 0:128], ALU.add))
                pl.wait("mm", "ve", vv)

        if first:
            stream_sets(w3_sets, w3_consume, wb=w3_wb)
        else:
            stream_sets_direct(w3_sets, w3_consume)

    _wb_done = [0]
    for tt in range(4):
        ffn0(tt)
        if tt == 0:
            _wb_done[0] = pl.c("sy")
    pl.barrier()

    if stage == "ffn0":
        return _finish(nc, pl, RESID, yout)

    for tile in range(NT):
        plan_ln(tile)
    pl.wait("ve", "ac")
    for c in range(8):
        v = pl.op("ve", lambda e, c=c: e.reduce_sum(
            gfcol[:, c:c + 1], XN[:, c * 2048:(c + 1) * 2048], axis=AXX))
    pl.wait("gp", "ve", v)
    gv = pl.dma("gp", cc_g_in, gfcol)
    pl.wait("gp", "gp", gv)
    ccg = pl.op("gp", lambda e: e.collective_compute(
        "AllGather", ALU.bypass, replica_groups=RG,
        ins=[cc_g_in.opt()], outs=[cc_g_out.opt()]), sem="cc")
    pl.barrier()

    def qk_sets(wname):
        return [[(DI[wname][oc, :, 0:1024], 1024, slabdst(oc, 0, 1024))]
                for oc in range(8)]

    _qevac = {}
    _qnv = [0]

    def q_consume(oc):
        for tc in range(4):
            key = oc * 4 + tc
            bank = PS[key % 4]
            if key >= 4:
                pl.wait("mm", "ac", _qevac[key - 4])
            mv = None
            for dc in range(8):
                f = _mmf(bank, slabdst(oc, dc * 128, 128),
                         XN[:, dc * 2048 + tc * 512:dc * 2048 + tc * 512 + 512],
                         dc == 0, dc == 7)
                if dc == 7:
                    mv = pl.op("mm", f)
                else:
                    pl.opn("mm", f)
            pl.wait("ac", "mm", mv)
            ev = pl.op("ac", lambda e, bank=bank, oc=oc, tc=tc: e.activation(
                RX[:, oc * 2048 + tc * 512:oc * 2048 + tc * 512 + 512],
                bank[:], ACT.Identity, bias=bqc[:, oc:oc + 1]))
            _qevac[key] = ev
            pl.wait("ve", "ac", ev)
            if _qnv[0]:
                pl.wait("ve", "mm", _qnv[0])
            sv = pl.op("ve", lambda e, oc=oc, tc=tc: e.tensor_tensor(
                scr_sq[:, 0:512],
                RX[:, oc * 2048 + tc * 512:oc * 2048 + tc * 512 + 512],
                RX[:, oc * 2048 + tc * 512:oc * 2048 + tc * 512 + 512],
                ALU.mult))
            pl.wait("mm", "ve", sv)
            pl.opn("mm", _mmf(PS[4][0:1, :], auxsb[:, 1:2],
                              scr_sq[:, 0:512], tc == 0, tc == 3))
            nv = pl.op("mm", _mmf(PS[5][0:1, :], auxsb[:, 2:3],
                                  scr_sq[:, 0:512], tc == 0, tc == 3))
            _qnv[0] = nv
            if tc == 3:
                pl.wait("ve", "mm", nv)
                pl.op("ve", lambda e, oc=oc: e.reduce_max(
                    qrow[0:1, 2 * oc:2 * oc + 1], PS[4][0:1, :], axis=AXX))
                rv = pl.op("ve", lambda e, oc=oc: e.reduce_max(
                    qrow[0:1, 2 * oc + 1:2 * oc + 2], PS[5][0:1, :], axis=AXX))
                pl.wait("mm", "ve", rv)

    _kdma, _ksq = {}, {}
    _knv = [0]

    def k_consume(oc):
        for tc in range(4):
            key = oc * 4 + tc
            bank = PS[key % 4]
            if key >= 4:
                pl.wait("mm", "d" + str(key % 2), _kdma[key - 4])
            mv = None
            for dc in range(8):
                f = _mmf(bank, slabdst(oc, dc * 128, 128),
                         XN[:, dc * 2048 + tc * 512:dc * 2048 + tc * 512 + 512],
                         dc == 0, dc == 7)
                if dc == 7:
                    mv = pl.op("mm", f)
                else:
                    pl.opn("mm", f)
            slot = scr_cast[:, (key % 2) * 512:(key % 2) * 512 + 512]
            pl.wait("ac", "mm", mv)
            if key >= 2:
                pl.wait("ac", "d" + str(key % 2), _kdma[key - 2])
                pl.wait("ac", "ve", _ksq[key - 2])
            ev = pl.op("ac", lambda e, bank=bank, oc=oc, slot=slot:
                       e.activation(slot, bank[:], ACT.Identity,
                                    bias=bkc[:, oc:oc + 1]))
            pl.wait("sy", "ac", ev)
            _kdma[key] = pl.dma("sy", cc_k_in[oc * 128:oc * 128 + 128,
                                              tc * 512:tc * 512 + 512], slot,
                                sem="d" + str(key % 2))
            pl.wait("ve", "ac", ev)
            if _knv[0]:
                pl.wait("ve", "mm", _knv[0])
            sv = pl.op("ve", lambda e, slot=slot: e.tensor_tensor(
                scr_sq[:, 512:1024], slot, slot, ALU.mult))
            _ksq[key] = sv
            pl.wait("mm", "ve", sv)
            pl.opn("mm", _mmf(PS[6][0:1, :], auxsb[:, 1:2],
                              scr_sq[:, 512:1024], tc == 0, tc == 3))
            nv = pl.op("mm", _mmf(PS[7][0:1, :], auxsb[:, 2:3],
                                  scr_sq[:, 512:1024], tc == 0, tc == 3))
            _knv[0] = nv
            if tc == 3:
                pl.wait("ve", "mm", nv)
                pl.op("ve", lambda e, oc=oc: e.reduce_max(
                    nrow[0:1, 2 * oc:2 * oc + 1], PS[6][0:1, :], axis=AXX))
                rv = pl.op("ve", lambda e, oc=oc: e.reduce_max(
                    nrow[0:1, 2 * oc + 1:2 * oc + 2], PS[7][0:1, :], axis=AXX))
                pl.wait("mm", "ve", rv)

    stream_sets(qk_sets("w1q"), q_consume)
    stream_sets(qk_sets("w1k"), k_consume)
    pl.barrier()

    bvex_sb = scr_pp
    bv_v = pl.dma("gp", bvex_sb, DI["bvex"])
    pl.wait("ve", "gp", bv_v)
    vsets = [[(DI["w1v"][dc, :, 0:1024], 1024,
               SLAB[:, dc * 1024:dc * 1024 + 1024]) for dc in range(8)]]
    _vdma = {}

    def v_consume(_si):
        for tile in range(NT):
            for oh in range(2):
                key = tile * 2 + oh
                bank = PS[key % 4]
                if key >= 4:
                    pl.wait("mm", "d" + str(key % 2), _vdma[key - 4])
                mv = None
                for dc in range(8):
                    f = _mmf(bank, XN[:, dc * 2048 + tile * 128:
                                      dc * 2048 + tile * 128 + 128],
                             SLAB[:, dc * 1024 + oh * 512:
                                  dc * 1024 + oh * 512 + 512],
                             dc == 0, dc == 7)
                    if dc == 7:
                        mv = pl.op("mm", f)
                    else:
                        pl.opn("mm", f)
                slot = scr_cast[:, (key % 2) * 512:(key % 2) * 512 + 512]
                pl.wait("ve", "mm", mv)
                if key >= 2:
                    pl.wait("ve", "d" + str(key % 2), _vdma[key - 2])
                ev = pl.op("ve", lambda e, bank=bank, oh=oh, slot=slot:
                           e.tensor_tensor(slot, bank[:],
                                           bvex_sb[:, oh * 512:oh * 512 + 512],
                                           ALU.add))
                pl.wait("sy", "ve", ev)
                _vdma[key] = pl.dma(
                    "sy", ccv_in[tile * 128:tile * 128 + 128,
                                  oh * 512:oh * 512 + 512], slot,
                    sem="d" + str(key % 2))

    stream_sets(vsets, v_consume)
    pl.barrier()

    ccK = pl.op("gp", lambda e: e.collective_compute(
        "AllGather", ALU.bypass, replica_groups=RG,
        ins=[cc_k_in.opt()], outs=[cc_k_out.opt()]), sem="cc")
    ccV = pl.op("gp", lambda e: e.collective_compute(
        "AllGather", ALU.bypass, replica_groups=RG,
        ins=[ccv_in.opt()], outs=[ccv_out.opt()]), sem="cc")
    pl.wait("gp", "ve", pl.c("ve"))
    gv = pl.dma("gp", cc_n_in, nrow)
    pl.wait("gp", "gp", gv)
    ccN = pl.op("gp", lambda e: e.collective_compute(
        "AllGather", ALU.bypass, replica_groups=RG,
        ins=[cc_n_in.opt()], outs=[cc_n_out.opt()]), sem="cc")

    if stage == "qkv1":
        return _finish(nc, pl, RESID, yout)

    pl.wait("gp", "cc", ccg)
    pl.dma("gp", gfcol, cc_g_out[0:128, :])
    gv = pl.dma("gp", scr_pp[:, 0:8], cc_g_out[128:256, :])
    pl.wait("ve", "gp", gv)
    pl.op("ve", lambda e: e.tensor_tensor(
        gfcol, gfcol, scr_pp[:, 0:8], ALU.add))
    v = pl.op("ve", lambda e: e.tensor_scalar(
        gfcol, gfcol, 1.0 / SEQ, None, ALU.mult))
    pl.wait("mm", "ve", v)

    def row_to_cols(src_row, n, dstcol):
        for c in range(n):
            tv = pl.op("mm", lambda e, c=c, src_row=src_row: e.transpose(
                PS[0][:, 0:1], src_row[0:1, c * 128:c * 128 + 128],
                identf[0:1, 0:1]))
            pl.wait("ve", "mm", tv)
            vv = pl.op("ve", lambda e, c=c, dstcol=dstcol: e.tensor_copy(
                dstcol[:, c:c + 1], PS[0][:, 0:1]))
            pl.wait("mm", "ve", vv)
        return vv

    v = pl.op("ve", lambda e: e.tensor_copy(gfcolb[:], gfcol[:]))
    cvv = None
    for nm, ncol, base_ in (("gc1", 512, 0), ("gc2", 256, 4096),
                            ("gc3", 1, 6144)):
        for dc in range(DI[nm].shape[0]):
            pl.wait("gp", "ve")
            lv = pl.dma("gp", LAND[:, (dc % 2) * 1024:(dc % 2) * 1024 + ncol],
                        DI[nm][dc, :, :])
            pl.wait("ve", "gp", lv)
            cvv = pl.op("ve", lambda e, dc=dc, ncol=ncol, base_=base_:
                        e.tensor_copy(
                            SLAB[:, base_ + dc * ncol:base_ + (dc + 1) * ncol],
                            LAND[:, (dc % 2) * 1024:(dc % 2) * 1024 + ncol]))
    pl.wait("mm", "ve", cvv)

    def gf_stage(nsl, rhs_of, width, brow_d, brow_n, actf):
        mv = None
        for dc in range(nsl):
            f = _mmf(PS[1][0:1, 0:width], gfcolb[:, dc:dc + 1], rhs_of(dc),
                     dc == 0, dc == nsl - 1)
            if dc == nsl - 1:
                mv = pl.op("mm", f)
            else:
                pl.opn("mm", f)
        pl.wait("gp", "ve")
        bv = pl.dma("gp", gfrow2[0:1, 0:brow_n], brow_d)
        pl.wait("ve", "mm", mv)
        pl.wait("ve", "gp", bv)
        v = pl.op("ve", lambda e, width=width: e.tensor_tensor(
            gfrow[0:1, 0:width], PS[1][0:1, 0:width], gfrow2[0:1, 0:width],
            ALU.add))
        pl.wait("ac", "ve", v)
        av = pl.op("ac", lambda e, width=width, actf=actf: e.activation(
            gfrow[0:1, 0:width], gfrow[0:1, 0:width], actf))
        pl.wait("mm", "ac", av)
        pl.wait("ve", "ac", av)

    gf_stage(8, lambda dc: SLAB[:, dc * 512:dc * 512 + 512], 512,
             DI["gb1"], 512, ACT.Gelu)
    v = row_to_cols(gfrow, 4, gfcol)
    pl.wait("ve", "ve", v)
    v = pl.op("ve", lambda e: e.tensor_copy(gfcolb[:, 0:4], gfcol[:, 0:4]))
    pl.wait("mm", "ve", v)
    gf_stage(4, lambda dc: SLAB[:, 4096 + dc * 256:4096 + dc * 256 + 256], 256,
             DI["gb2"], 256, ACT.Gelu)
    v = row_to_cols(gfrow, 2, gfcol)
    pl.wait("ve", "ve", v)
    v = pl.op("ve", lambda e: e.tensor_copy(gfcolb[:, 0:2], gfcol[:, 0:2]))
    pl.wait("mm", "ve", v)
    gf_stage(2, lambda dc: SLAB[:, 6144 + dc:6144 + dc + 1], 1,
             DI["gb3"], 1, ACT.Sigmoid)
    pl.op("ve", lambda e: e.tensor_copy(gfb[0:1, 0:1], gfrow[0:1, 0:1]))
    pl.op("ve", lambda e: e.tensor_tensor(
        gfrow[0:1, 1:2], gfrow[0:1, 0:1], gfb[0:1, 0:1], ALU.subtract))
    v = pl.op("ve", lambda e: e.tensor_copy(gfb[0:1, 1:2], gfrow[0:1, 1:2]))
    pl.wait("mm", "ve", v)
    pl.opn("mm", _mmf(PS[1][:, 0:1], crow[:], gfb[0:1, 0:1], True, False))
    mv = pl.op("mm", _mmf(PS[1][:, 0:1], crow[:], gfb[0:1, 1:2], False, True))
    pl.wait("ve", "mm", mv)
    pl.op("ve", lambda e: e.tensor_copy(gfc[:], PS[1][:, 0:1]))
    pl.op("ve", lambda e: e.tensor_tensor(acol[:], arc[:], gfc[:], ALU.mult))
    pl.op("ve", lambda e: e.tensor_scalar(acol[:], acol[:], 1.4, None,
                                          ALU.mult))
    pl.op("ve", lambda e: e.tensor_tensor(acol[:], acol[:], arc[:],
                                          ALU.subtract))
    pl.op("ve", lambda e: e.tensor_scalar(acol[:], acol[:], 1.0, 0.1,
                                          ALU.mult, ALU.add))
    pl.op("ve", lambda e: e.tensor_scalar(bcol[:], arc[:], 0.6, None,
                                          ALU.mult))
    for dc in range(2):
        pl.wait("gp", "ve")
        lv = pl.dma("gp", LAND[:, 0:1], DI["cn4"][dc, :, :])
        pl.wait("ve", "gp", lv)
        cvv = pl.op("ve", lambda e, dc=dc: e.tensor_copy(cn4b[:, dc:dc + 1],
                                                         LAND[:, 0:1]))
        pl.wait("gp", "ve", cvv)
    pl.barrier()

    cn_desc = []
    for oc in range(8):
        for hf in range(2):
            cn_desc.append(("cn1", oc, hf, (oc * 2 + hf) * 512))
    for oc in range(4):
        for hf in range(2):
            cn_desc.append(("cn2", oc, hf, 8192 + (oc * 2 + hf) * 512))
    for oc in range(2):
        cn_desc.append(("cn3", oc, 0, 12288 + oc * 512))
    cn_sets = [[(DI[nm][oc, :, hf * 512:hf * 512 + 512], 512,
                 XN[:, xo:xo + 512])] for (nm, oc, hf, xo) in cn_desc]

    def cn_cache_consume(si):
        pl.op("mm", lambda e: e.nop())

    stream_sets(cn_sets, cn_cache_consume, land_base=2048, land_slots=2,
                land_cols=512)
    pl.barrier()

    def ffn1_eighth(egt):
        sreg = LAND[:, 0:2048]
        pl.wait("ve", "mm")
        v = None
        for lt in range(2):
            tile = egt * 2 + lt
            v = pl.op("ve", lambda e, tile=tile, lt=lt: e.tensor_scalar(
                sreg[:, lt * D:lt * D + D],
                RESID[:, tile * D:tile * D + D],
                rcol[:, tile:tile + 1], mrcol[:, tile:tile + 1],
                ALU.mult, ALU.subtract))
        pl.wait("ve", "ve", v)
        for cyc in range(3):
            pl.wait("mm", "ve")
            evd = {}
            for lt in range(2):
                for c in range(8):
                    k = lt * 8 + c
                    bank = PS[6 + (k % 2)]
                    if k >= 2:
                        pl.wait("mm", "ac", evd[k - 2])
                    tvl = pl.op("mm", lambda e, lt=lt, c=c, bank=bank:
                                e.transpose(bank[:, 0:128],
                                            sreg[:, lt * D + c * 128:
                                                 lt * D + c * 128 + 128],
                                            identf[:]))
                    pl.wait("ac", "mm", tvl)
                    evd[k] = pl.op("ac", lambda e, lt=lt, c=c, bank=bank:
                                   e.activation(
                                       SLAB[:, c * 256 + lt * 128:
                                            c * 256 + lt * 128 + 128],
                                       bank[:, 0:128], ACT.Identity))
            pl.wait("mm", "ac")

            _cne = {}
            _stage_last = {}
            desc2 = [("cn1", 8, 2, 8, 0, 2048, b1c),
                     ("cn2", 4, 2, 8, 2048, 4096, b2c),
                     ("cn3", 2, 1, 4, 4096, 5120, b3c)]
            for (nm, noc, nh, nd, srcb, dstb, bias) in desc2:
                for oc in range(noc):
                    bank = PS[oc % 4]
                    gk = ("bank", nm, oc % 4)
                    if gk in _cne:
                        pl.wait("mm", "ac", _cne[gk])
                    if oc == 0 and nm == "cn2":
                        pl.wait("mm", "ac", _stage_last["cn1"])
                    if oc == 0 and nm == "cn3":
                        pl.wait("mm", "ac", _stage_last["cn2"])
                    if nm == "cn1":
                        xbase = (oc * 2) * 512
                    elif nm == "cn2":
                        xbase = 8192 + (oc * 2) * 512
                    else:
                        xbase = 12288 + oc * 512
                    mv = None
                    for dc in range(nd):
                        f = _mmf(bank[:, 0:256],
                                 XN[:, xbase + dc * 128:xbase + dc * 128 + 128],
                                 SLAB[:, srcb + dc * 256:srcb + dc * 256 + 256],
                                 dc == 0, dc == nd - 1)
                        if dc == nd - 1:
                            mv = pl.op("mm", f)
                        else:
                            pl.opn("mm", f)
                    pl.wait("ac", "mm", mv)
                    av = pl.op("ac", lambda e, bank=bank, oc=oc, dstb=dstb,
                               bias=bias: e.activation(
                                   SLAB[:, dstb + oc * 256:
                                        dstb + oc * 256 + 256],
                                   bank[:, 0:256], ACT.Gelu,
                                   bias=bias[:, oc:oc + 1]))
                    _cne[gk] = av
                    _stage_last[nm] = av
            pl.wait("mm", "ac")
            for lt in range(2):
                bank = PS[4 + lt % 2]
                mv = None
                for dc in range(2):
                    f = _mmf(bank[:, 0:1],
                             SLAB[:, 5120 + dc * 256 + lt * 128:
                                  5120 + dc * 256 + lt * 128 + 128],
                             cn4b[:, dc:dc + 1], dc == 0, dc == 1)
                    if dc == 1:
                        mv = pl.op("mm", f)
                    else:
                        pl.opn("mm", f)
                pl.wait("ac", "mm", mv)
                av = pl.op("ac", lambda e, bank=bank: e.activation(
                    tfcol[:], bank[:, 0:1], ACT.Sigmoid, bias=b4c[:, 0:1]))
                pl.wait("ve", "ac", av)
                pl.op("ve", lambda e, lt=lt: e.tensor_scalar(
                    ccols[:, lt:lt + 1], tfcol[:], bcol[:, 0:1], acol[:, 0:1],
                    ALU.mult, ALU.add))
                pl.op("ve", lambda e, lt=lt: e.tensor_scalar(
                    ncols_[:, lt:lt + 1], ccols[:, lt:lt + 1], -1.0, None,
                    ALU.mult))
                st = sreg[:, lt * D:lt * D + D]
                pl.op("ve", lambda e, st=st: e.tensor_copy(
                    scr_silu0[:, 0:256], st[:, 0:256]))
                pl.op("ve", lambda e, st=st: e.tensor_copy(
                    scr_silu0[:, 256:512], st[:, 256:512]))
                pl.op("ve", lambda e, st=st, lt=lt: e.scalar_tensor_tensor(
                    st[:, 0:256], st[:, 512:768], ccols[:, lt:lt + 1],
                    st[:, 0:256], ALU.mult, ALU.add))
                pl.op("ve", lambda e, st=st, lt=lt: e.scalar_tensor_tensor(
                    st[:, 256:512], st[:, 768:1024], ncols_[:, lt:lt + 1],
                    st[:, 256:512], ALU.mult, ALU.add))
                pl.op("ve", lambda e, st=st, lt=lt: e.scalar_tensor_tensor(
                    st[:, 512:768], scr_silu0[:, 0:256], ncols_[:, lt:lt + 1],
                    st[:, 512:768], ALU.mult, ALU.add))
                v = pl.op("ve", lambda e, st=st, lt=lt: e.scalar_tensor_tensor(
                    st[:, 768:1024], scr_silu0[:, 256:512], ccols[:, lt:lt + 1],
                    st[:, 768:1024], ALU.mult, ALU.add))
                pl.wait("mm", "ve", v)
        for lt in range(2):
            tile = egt * 2 + lt
            pl.op("ve", lambda e, tile=tile: e.tensor_scalar(
                scr_pp, RESID[:, tile * D:tile * D + D],
                rcol[:, tile:tile + 1], mrcol[:, tile:tile + 1],
                ALU.mult, ALU.subtract))
            pl.op("ve", lambda e, tile=tile: e.tensor_tensor(
                RESID[:, tile * D:tile * D + D],
                RESID[:, tile * D:tile * D + D], scr_pp, ALU.add))
            pl.op("ve", lambda e, tile=tile, lt=lt: e.tensor_tensor(
                RESID[:, tile * D:tile * D + D],
                RESID[:, tile * D:tile * D + D],
                sreg[:, lt * D:lt * D + D], ALU.add))

    for egt in range(8):
        ffn1_eighth(egt)
    pl.barrier()

    if stage == "ffn1":
        return _finish(nc, pl, RESID, yout)

    pl.wait("gp", "cc", ccN)
    pl.dma("gp", qkn[0:1, 32:48], cc_n_out[0:1, :])
    gv = pl.dma("gp", qkn[0:1, 48:64], cc_n_out[1:2, :])
    pl.wait("ve", "gp", gv)
    pl.op("ve", lambda e: e.tensor_tensor(
        qkn[0:1, 32:48], qkn[0:1, 32:48], qkn[0:1, 48:64], ALU.max))
    v = pl.op("ve", lambda e: e.tensor_tensor(
        qkn[0:1, 32:48], qkn[0:1, 32:48], qrow, ALU.mult))
    pl.wait("ac", "ve", v)
    av = pl.op("ac", lambda e: e.activation(qkn[0:1, 32:48], qkn[0:1, 32:48],
                                            ACT.Sqrt))
    pl.wait("ve", "ac", av)
    pl.op("ve", lambda e: e.tensor_scalar(
        qkn[0:1, 32:48], qkn[0:1, 32:48], -1.01 / 32.0, None, ALU.mult))
    v = pl.op("ve", lambda e: e.tensor_copy(negmb[:], qkn[0:1, 32:48]))
    pl.wait("mm", "ve", v)
    mv = pl.op("mm", _mmf(PS[0][:, 0:16], crow[:], negmb[:], True, True))
    pl.wait("ve", "mm", mv)
    v = pl.op("ve", lambda e: e.tensor_copy(negm[:], PS[0][:, 0:16]))
    pl.wait("ac", "ve", v)
    pl.wait("gp", "cc", pl.c("cc"))
    pl.barrier()

    VSL = (4096, 8320)
    E0B, E1B = 12544, 13568
    _ocs = [VSL[s_] + kt * 132 + off
            for s_ in (0, 1) for kt in range(32) for off in (64, 130)]
    for c0 in _ocs[:-1]:
        pl.opn("ve", lambda e, c0=c0: e.tensor_copy(XN[:, c0:c0 + 2],
                                                    ones2[:]))
    c0 = _ocs[-1]
    msv = pl.op("ve", lambda e, c0=c0: e.tensor_copy(XN[:, c0:c0 + 2],
                                                     ones2[:]))

    _expd, _pvd, _hpdone = {}, {}, {}
    _kv, _vv = {}, {}
    _evacA, _tailv, _zbfv, _expn, _normv = {}, {}, {}, {}, {}
    zhiAB = scr_sq[0:1, 0:1024]
    zloAB = scr_cast[0:1, 0:1024]
    zhiA, zhiB = scr_sq[0:1, 0:512], scr_sq[0:1, 512:1024]
    zloA, zloB = scr_cast[0:1, 0:512], scr_cast[0:1, 512:1024]
    zrowAB = SCRF[0:1, 0:1024]
    zinvAB = SCRF[0:1, 1024:2048]
    LSL = 2048

    def issue_V(hp):
        pl.wait("gp", "mm", _hpdone.get(hp - 2, 0))
        if hp < 2:
            pl.wait("gp", "ve", msv)
        vsl = VSL[hp % 2]
        v = 0
        for kt in range(32):
            pl.dma("gp", XN[:, vsl + kt * 132:vsl + kt * 132 + 64],
                   ccv_out[kt * 128:kt * 128 + 128,
                            hp * 128:hp * 128 + 64])
            v = pl.dma("gp", XN[:, vsl + kt * 132 + 66:vsl + kt * 132 + 130],
                       ccv_out[kt * 128:kt * 128 + 128,
                                hp * 128 + 64:hp * 128 + 128])
        _vv[hp] = v

    def issue_K(hp):
        pl.wait("sy", "mm", _hpdone.get(hp - 1, 0))
        v0 = pl.dma("sy", XN[:, 0:2048], cc_k_out[hp * 128:hp * 128 + 128, :])
        v1 = pl.dma("sy", XN[:, 2048:4096],
                    cc_k_out[D + hp * 128:D + hp * 128 + 128, :])
        _kv[hp] = (v0, v1)

    def eslot(k):
        e0 = (scr_sq[:, (k % 2) * 512:(k % 2) * 512 + 512] if k % 4 < 2 else
              XN[:, E0B + (k % 2) * 512:E0B + (k % 2) * 512 + 512])
        e1 = (scr_cast[:, (k % 2) * 512:(k % 2) * 512 + 512] if k % 4 < 2 else
              XN[:, E1B + (k % 2) * 512:E1B + (k % 2) * 512 + 512])
        return e0, e1

    def emit_expansion(qci):
        B = PS[6 + (qci % 2)]
        pl.wait("mm", "ve", _zbfv[qci])
        if qci - 2 in _normv:
            pl.wait("mm", "ve", _normv[qci - 2])
        pl.opn("mm", _mmf(B, sel2sb[0:1, :], zhiA, True, False))
        pl.opn("mm", _mmf(B, selb[:], zhiB, False, False))
        pl.opn("mm", _mmf(B, sel2sb[0:1, :], zloA, False, False))
        ev = pl.op("mm", _mmf(B, selb[:], zloB, False, True))
        _expn[qci] = ev
        s = LSL + (qci % 2) * 512
        ucol = (qci // 4) * 2048 + (qci % 4) * 512
        pl.wait("ve", "mm", ev)
        pl.op("ve", lambda e, B=B, s=s, ucol=ucol: e.tensor_tensor(
            RY[0:64, ucol:ucol + 512], LAND[0:64, s:s + 512], B[0:64, :],
            ALU.mult))
        v = pl.op("ve", lambda e, B=B, s=s, ucol=ucol: e.tensor_tensor(
            RY[64:128, ucol:ucol + 512], LAND[64:128, s:s + 512],
            B[64:128, :], ALU.mult))
        _normv[qci] = v

    issue_V(0)
    issue_V(1)
    issue_K(0)

    wo_ready = 0
    for cci in range(8):
        pl.wait("gp", "ve")
        lv = pl.dma("gp", LAND[:, (cci % 2) * 1024:(cci % 2) * 1024 + 1024],
                    DI["w1o"][cci, :, :])
        pl.wait("ve", "gp", lv)
        wo_ready = pl.op("ve", lambda e, cci=cci: e.tensor_copy(
            SLAB[:, cci * 1024:cci * 1024 + 1024],
            LAND[:, (cci % 2) * 1024:(cci % 2) * 1024 + 1024]))

    for hp in range(8):
        vsl = VSL[hp % 2]
        pl.wait("mm", "sy", _kv[hp][0])
        pl.wait("mm", "gp", _vv[hp])
        for qc in range(4):
            qci = hp * 4 + qc
            qs0 = RX[0:64, hp * 2048 + qc * 512:hp * 2048 + qc * 512 + 512]
            qs1 = RX[64:128, hp * 2048 + qc * 512:hp * 2048 + qc * 512 + 512]

            def scores(k):
                sb0 = PS[(k % 2) * 2]
                sb1 = PS[(k % 2) * 2 + 1]
                ks0 = XN[0:64, k * 128:k * 128 + 128]
                ks1 = XN[64:128, k * 128:k * 128 + 128]
                pl.opn("mm", _mmf(sb0, ks0, qs0, True, True))
                return pl.op("mm", _mmf(sb1, ks1, qs1, True, True,
                                        tp=(64, 0))), sb0, sb1

            def expk(k, smv, sb0, sb1, extra=0):
                e0, e1 = eslot(k)
                pl.wait("ac", "mm", smv)
                if extra:
                    pl.wait("ac", "mm", extra)
                if k >= 4:
                    pl.wait("ac", "mm", _pvd[k - 4])
                pl.opn("ac", lambda e, sb0=sb0, e0=e0, hp=hp: e.activation(
                    e0, sb0[:], ACT.Exp, scale=1.0 / 32.0,
                    bias=negm[:, 2 * hp:2 * hp + 1]))
                ev = pl.op("ac", lambda e, sb1=sb1, e1=e1, hp=hp:
                           e.activation(
                               e1, sb1[:], ACT.Exp, scale=1.0 / 32.0,
                               bias=negm[:, 2 * hp + 1:2 * hp + 2]))
                _expd[k] = ev

            e31p = _expd.get(31, 0)
            if e31p:
                pl.wait("mm", "ac", e31p)
            smv0, sb00, sb01 = scores(0)
            smv1, sb10, sb11 = scores(1)
            if qci > 0:
                emit_expansion(qci - 1)
            expk(0, smv0, sb00, sb01, extra=_expn.get(qci - 1, 0))
            expk(1, smv1, sb10, sb11, extra=_expn.get(qci - 1, 0))

            pvv = None
            for step in range(2, 34):
                k = step
                if k < 32:
                    if qc == 0 and k == 16:
                        pl.wait("mm", "sy", _kv[hp][1])
                    pl.wait("mm", "ac", _expd[k - 2])
                    smv, sb0, sb1 = scores(k)
                pk = step - 2
                if pk == 0 and qci > 0:
                    pl.wait("mm", "ac", _evacA[qci - 1])
                    pl.wait("mm", "ve", _tailv[qci - 1])
                pl.wait("mm", "ac", _expd[pk])
                p0, p1 = eslot(pk)
                pl.opn("mm", _mmf(PS[4][0:65, :],
                                  XN[:, vsl + pk * 132:vsl + pk * 132 + 65],
                                  p0, pk == 0, pk == 31))
                pvv = pl.op("mm", _mmf(
                    PS[5][0:65, :],
                    XN[:, vsl + pk * 132 + 66:vsl + pk * 132 + 131],
                    p1, pk == 0, pk == 31))
                _pvd[pk] = pvv
                if k < 32:
                    expk(k, smv, sb0, sb1)

            s = LSL + (qci % 2) * 512
            pl.wait("ac", "mm", pvv)
            if qci - 2 in _normv:
                pl.wait("ac", "ve", _normv[qci - 2])
            ev = pl.op("ac", lambda e, s=s: e.activation(
                LAND[0:64, s:s + 512], PS[4][0:64, :], ACT.Identity))
            _evacA[qci] = ev
            pl.wait("ve", "mm", pvv)
            pl.op("ve", lambda e, s=s: e.tensor_copy(
                LAND[64:128, s:s + 512], PS[5][0:64, :]))
            pl.op("ve", lambda e: e.tensor_copy(zrowA, PS[4][64:65, :]))
            tv = pl.op("ve", lambda e: e.tensor_copy(zrowB, PS[5][64:65, :]))
            _tailv[qci] = tv
            pl.op("ve", lambda e: e.reciprocal(zinvAB, zrowAB))
            pl.op("ve", lambda e: e.tensor_copy(zhiAB, zinvAB))
            pl.op("ve", lambda e: e.tensor_tensor(
                zrowAB, zinvAB, zhiAB, ALU.subtract))
            zv = pl.op("ve", lambda e: e.tensor_copy(zloAB, zrowAB))
            _zbfv[qci] = zv
        _hpdone[hp] = pl.c("mm")
        if hp + 2 < 8:
            issue_V(hp + 2)
        if hp + 1 < 8:
            issue_K(hp + 1)
    emit_expansion(31)
    pl.barrier()

    bo_v = pl.dma("gp", scr_pp, DI["boex"])
    pl.wait("ve", "gp", bo_v)
    pl.wait("mm", "ve", wo_ready)
    _opv = {}
    for qt in range(NT):
        for eh in range(2):
            key = qt * 2 + eh
            bank = PS[key % 4]
            if key >= 4:
                pl.wait("mm", "ve", _opv[key - 4])
            mv = None
            for cci in range(8):
                f = _mmf(bank, RY[:, cci * 2048 + qt * 128:
                                  cci * 2048 + qt * 128 + 128],
                         SLAB[:, cci * 1024 + eh * 512:
                              cci * 1024 + eh * 512 + 512],
                         cci == 0, cci == 7)
                if cci == 7:
                    mv = pl.op("mm", f)
                else:
                    pl.opn("mm", f)
            pl.wait("ve", "mm", mv)
            pl.op("ve", lambda e, bank=bank, eh=eh: e.tensor_tensor(
                scr_silu0, bank[:], scr_pp[:, eh * 512:eh * 512 + 512],
                ALU.add))
            v = pl.op("ve", lambda e, qt=qt, eh=eh: e.tensor_tensor(
                RESID[:, qt * D + eh * 512:qt * D + eh * 512 + 512],
                RESID[:, qt * D + eh * 512:qt * D + eh * 512 + 512],
                scr_silu0, ALU.add))
            _opv[key] = v
    pl.barrier()

    return _finish(nc, pl, RESID, yout)


def _finish(nc, pl, RESID, yout):
    pl.barrier()
    last = 0
    for tile in range(NT):
        last = pl.dma("sy", yout[tile * 128:(tile + 1) * 128, :],
                      RESID[:, tile * D:(tile + 1) * D])
    pl.wait("sy", "sy", last)
    pl.wait("gp", "cc", pl.c("cc"))
    _emit(nc, pl)
    return nc


def _emit(nc, pl):
    import contextlib
    with contextlib.ExitStack() as stack:
        S = {s: stack.enter_context(nc.semaphore(f"sem_{s}")) for s in SEMS}
        with nc.Block() as block:
            @block.sync
            def _(eng):
                for f in pl.q["sy"]:
                    f(eng, S)

            @block.gpsimd
            def _(eng):
                for f in pl.q["gp"]:
                    f(eng, S)

            @block.scalar
            def _(eng):
                for f in pl.q["ac"]:
                    f(eng, S)

            @block.vector
            def _(eng):
                for f in pl.q["ve"]:
                    f(eng, S)

            @block.tensor
            def _(eng):
                for f in pl.q["mm"]:
                    f(eng, S)



def kernel(**inputs):
    stage = DEBUG_STAGE or "full"
    nc = build_nc(stage)
    in_maps = make_in_maps(inputs)
    res = run_bass_kernel_spmd(nc, in_maps, core_ids=list(range(NCORE)))
    out = np.empty((4, SEQ, D), dtype=np.float32)
    for c in range(NCORE):
        b, h = c // 2, c % 2
        out[b, h * T:(h + 1) * T, :] = res.results[c]["y"]
    return out


# version 14 (final)
# speedup vs baseline: 1.0380x; 1.0380x over previous
import numpy as np
import ml_dtypes
import concourse.bass as bass
import concourse.mybir as mybir
from concourse.bass_utils import run_bass_kernel_spmd

F32 = mybir.dt.float32
BF16 = mybir.dt.bfloat16
ACT = mybir.ActivationFunctionType
ALU = mybir.AluOpType
AXX = mybir.AxisListType.X
BF = ml_dtypes.bfloat16

NCORE = 8
T = 2048
NT = 16
D = 1024
NH = 16
SEQ = 4096
RG = [[0, 1], [2, 3], [4, 5], [6, 7]]

DEBUG_STAGE = None

ENGS = ("sy", "gp", "ac", "ve", "mm")
SEMS = ENGS + ("wc", "cc", "d0", "d1", "s0", "s1", "s2")


class Plan:
    def __init__(self):
        self.q = {e: [] for e in ENGS}
        self.cnt = {s: 0 for s in SEMS}
        self.waited = {s: 0 for s in SEMS}
        self.selfw = {s: 0 for s in SEMS}

    def c(self, s):
        return self.cnt[s]

    def add(self, s, n=1):
        self.cnt[s] += n
        return self.cnt[s]

    def on(self, e, f):
        self.q[e].append(f)

    def wait(self, e, s, v=None):
        if v is None:
            v = self.cnt[s]
        if v <= 0:
            return
        if s == e and v > self.selfw[s]:
            self.selfw[s] = v
        self.waited[s] = max(self.waited[s], v)
        self.on(e, lambda eng, S, s=s, v=v: eng.wait_ge(S[s], v))

    def _pre_dma(self, e):
        w = self.waited[e]
        if w > self.selfw[e]:
            self.selfw[e] = w
            self.on(e, lambda eng, S, e=e, w=w: eng.wait_ge(S[e], w))

    def dma(self, e, out_ap, in_ap, sem=None):
        self._pre_dma(e)
        s = sem or e
        v = self.add(s, 16)
        self.on(e, lambda eng, S, o=out_ap, i=in_ap, s=s:
                eng.dma_start(out=o, in_=i).then_inc(S[s], 16))
        return v

    def dmat(self, e, out_ap, in_ap):
        self._pre_dma(e)
        v = self.add(e, 16)
        self.on(e, lambda eng, S, o=out_ap, i=in_ap, e=e:
                eng.dma_start_transpose(o, i).then_inc(S[e], 16))
        return v

    def op(self, e, f, sem=None):
        if e in ("ve", "ac") and self.cnt[e] > self.selfw[e]:
            self.wait(e, e)
        s = sem or e
        v = self.add(s, 1)
        self.on(e, lambda eng, S, f=f, s=s: f(eng).then_inc(S[s], 1))
        return v

    def opn(self, e, f):
        self.on(e, lambda eng, S, f=f: f(eng))

    def barrier(self):
        snap = dict(self.cnt)
        for e in ENGS:
            for s in SEMS:
                if snap[s] > 0 and (s != e or e in ("sy", "gp")):
                    self.wait(e, s, snap[s])


def _ap(x):
    return x if isinstance(x, bass.AP) else x[:]


def _mmf(ps, lhsT, rhs, start, stop, tp=None):
    ps, lhsT, rhs = _ap(ps), _ap(lhsT), _ap(rhs)

    def f(eng):
        return eng.matmul(ps, lhsT, rhs, start=start, stop=stop,
                          tile_position=tp, skip_group_check=True)
    return f



def _pack_lhsT(w, ocs=128):
    O, Din = w.shape
    noc, ndc = O // ocs, Din // 128
    r = w.reshape(noc, ocs, ndc, 128).transpose(0, 3, 2, 1)
    return np.ascontiguousarray(r.reshape(noc, 128, ndc * ocs))


def _pack_rhs(w):
    Din, O = w.shape
    return np.ascontiguousarray(w.reshape(Din // 128, 128, O))


def _pp(vec, ncols):
    return np.ascontiguousarray(np.asarray(vec).reshape(ncols, 128).T)


def make_in_maps(inp):
    f32 = lambda a: np.ascontiguousarray(np.asarray(a, dtype=np.float32))
    x_full = f32(inp["L0_attn_x"])
    wqkv0 = f32(inp["L0_attn_wqkv"])
    perm = np.concatenate([np.arange(192 * h + 64 * j, 192 * h + 64 * (j + 1))
                           for j in range(3) for h in range(NH)])
    w1qkv = f32(inp["L1_attn_w_qkv"])
    shared = {
        "w0qkv": _pack_rhs(np.ascontiguousarray(wqkv0[:, perm])),
        "w0w1": _pack_lhsT(f32(inp["L0_ffn_w1_w"])),
        "w0w2": _pack_lhsT(f32(inp["L0_ffn_w2_w"])),
        "w0h1": _pack_lhsT(f32(inp["L0_ffn_h1_w"])),
        "w0h2": _pack_lhsT(f32(inp["L0_ffn_h2_w"])),
        "w0w3": _pack_lhsT(f32(inp["L0_ffn_w3_w"])),
        "s1pp": _pp(f32(inp["L0_ffn_w1_s"]), 32),
        "s2pp": _pp(f32(inp["L0_ffn_w2_s"]), 32),
        "hs1pp": _pp(f32(inp["L0_ffn_h1_s"]), 32),
        "hs2pp": _pp(f32(inp["L0_ffn_h2_s"]), 32),
        "s3pp": _pp(f32(inp["L0_ffn_w3_s"]), 8),
        "w1q": _pack_lhsT(w1qkv[0:1024]),
        "w1k": _pack_lhsT(w1qkv[1024:2048]),
        "w1v": _pack_rhs(np.ascontiguousarray(w1qkv[2048:3072].T)),
        "bqpp": _pp(f32(inp["L1_attn_b_qkv"])[0:1024], 8),
        "bkpp": _pp(f32(inp["L1_attn_b_qkv"])[1024:2048], 8),
        "bvex": np.ascontiguousarray(np.broadcast_to(
            f32(inp["L1_attn_b_qkv"])[2048:3072], (128, D))),
        "w1o": _pack_rhs(np.ascontiguousarray(f32(inp["L1_attn_w_o"]).T)),
        "boex": np.ascontiguousarray(np.broadcast_to(
            f32(inp["L1_attn_b_o"]), (128, D))),
        "cn1": _pack_lhsT(f32(inp["L1_ffn_cn_w1"]).T),
        "cn2": _pack_lhsT(f32(inp["L1_ffn_cn_w2"]).T),
        "cn3": _pack_lhsT(f32(inp["L1_ffn_cn_w3"]).T),
        "cn4": _pack_rhs(f32(inp["L1_ffn_cn_w4"])),
        "b1pp": _pp(f32(inp["L1_ffn_cn_b1"]), 8),
        "b2pp": _pp(f32(inp["L1_ffn_cn_b2"]), 4),
        "b3pp": _pp(f32(inp["L1_ffn_cn_b3"]), 2),
        "b4ex": np.ascontiguousarray(np.broadcast_to(
            f32(inp["L1_ffn_cn_b4"]), (128, 1))),
        "gc1": _pack_rhs(f32(inp["L1_ffn_gc_w1"])),
        "gc2": _pack_rhs(f32(inp["L1_ffn_gc_w2"])),
        "gc3": _pack_rhs(f32(inp["L1_ffn_gc_w3"])),
        "gb1": f32(inp["L1_ffn_gc_b1"]).reshape(1, 512),
        "gb2": f32(inp["L1_ffn_gc_b2"]).reshape(1, 256),
        "gb3": f32(inp["L1_ffn_gc_b3"]).reshape(1, 1),
        "arex": np.ascontiguousarray(np.broadcast_to(
            f32(inp["L1_ffn_adaptive_range"]), (128, 1))),
    }
    maps = []
    for c in range(NCORE):
        b, h = c // 2, c % 2
        gidx = np.arange(T, dtype=np.float64) + h * T
        mx = np.maximum(gidx, 1.0)
        neg_lni = -np.log(mx)
        neg_lni[gidx < 1] = 40.0
        m = dict(shared)
        m["x0"] = np.ascontiguousarray(x_full[b, h * T:(h + 1) * T, :])
        m["neglni"] = _pp(neg_lni.astype(np.float32), 16)
        m["invtm"] = _pp((1.0 / mx).astype(np.float32), 16)
        m["cflag"] = np.full((1, 1), float(h), dtype=np.float32)
        maps.append(m)
    return maps



_NC_CACHE = {}


def build_nc(stage):
    nc = bass.Bass("TRN2", target_bir_lowering=False)
    pl = Plan()

    DI = {}
    for name, shape in (
        ("x0", (T, D)), ("w0qkv", (8, 128, 3072)), ("w0w1", (32, 128, 1024)),
        ("w0w2", (32, 128, 1024)), ("w0h1", (32, 128, 4096)),
        ("w0h2", (32, 128, 4096)), ("w0w3", (8, 128, 4096)),
        ("s1pp", (128, 32)), ("s2pp", (128, 32)), ("hs1pp", (128, 32)),
        ("hs2pp", (128, 32)), ("s3pp", (128, 8)),
        ("w1q", (8, 128, 1024)), ("w1k", (8, 128, 1024)),
        ("w1v", (8, 128, 1024)), ("bqpp", (128, 8)), ("bkpp", (128, 8)),
        ("bvex", (128, D)), ("w1o", (8, 128, 1024)), ("boex", (128, D)),
        ("cn1", (8, 128, 1024)), ("cn2", (4, 128, 1024)),
        ("cn3", (2, 128, 512)), ("cn4", (2, 128, 1)),
        ("b1pp", (128, 8)), ("b2pp", (128, 4)), ("b3pp", (128, 2)),
        ("b4ex", (128, 1)), ("gc1", (8, 128, 512)), ("gc2", (4, 128, 256)),
        ("gc3", (2, 128, 1)), ("gb1", (1, 512)), ("gb2", (1, 256)),
        ("gb3", (1, 1)), ("arex", (128, 1)), ("neglni", (128, 16)),
        ("invtm", (128, 16)), ("cflag", (1, 1)),
    ):
        DI[name] = nc.dram_tensor(name, list(shape), F32,
                                  kind="ExternalInput").ap()

    yout = nc.dram_tensor("y", [T, D], F32, kind="ExternalOutput").ap()

    cc_a_in = nc.dram_tensor("cc_a_in", [1, D], F32, kind="Internal").ap()
    cc_a_out = nc.dram_tensor("cc_a_out", [2, D], F32, kind="Internal").ap()
    cc_g_in = nc.dram_tensor("cc_g_in", [128, 8], F32, kind="Internal").ap()
    cc_g_out = nc.dram_tensor("cc_g_out", [256, 8], F32, kind="Internal").ap()
    cc_k_in = nc.dram_tensor("cc_k_in", [D, T], BF16, kind="Internal").ap()
    cc_k_out = nc.dram_tensor("cc_k_out", [2 * D, T], BF16, kind="Internal").ap()
    ccv_in = nc.dram_tensor("ccv_in", [T, D], BF16, kind="Internal").ap()
    ccv_out = nc.dram_tensor("ccv_out", [2 * T, D], BF16, kind="Internal").ap()
    cc_n_in = nc.dram_tensor("cc_n_in", [1, NH], F32, kind="Internal").ap()
    cc_n_out = nc.dram_tensor("cc_n_out", [2, NH], F32, kind="Internal").ap()

    SC = {}
    for nm, shp in (("w0w1", (32, 128, 1024)), ("w0w2", (32, 128, 1024)),
                    ("w0h1", (32, 128, 4096)), ("w0h2", (32, 128, 4096)),
                    ("w0w3", (8, 128, 4096))):
        SC[nm] = nc.dram_tensor("sc_" + nm, list(shp), BF16,
                                kind="Internal").ap()

    identf_d = nc.inline_tensor(np.eye(128, dtype=np.float32), "identf_d").ap()
    lstrict_d = nc.inline_tensor(
        (np.arange(128)[:, None] < np.arange(128)[None, :]).astype(BF),
        "lstrict_d").ap()
    aux_np = np.zeros((128, 8), dtype=BF)
    aux_np[:, 0] = 1
    aux_np[0:64, 1] = 1
    aux_np[64:128, 2] = 1
    aux_d = nc.inline_tensor(aux_np, "aux_d").ap()
    crow_d = nc.inline_tensor(np.ones((1, 128), dtype=BF), "crow_d").ap()
    ones2_d = nc.inline_tensor(np.ones((128, 2), dtype=BF), "ones2_d").ap()
    sel2_np = np.zeros((2, 128), dtype=BF)
    sel2_np[0, 0:64] = 1
    sel2_np[1, 64:128] = 1
    sel2_d = nc.inline_tensor(sel2_np, "sel2_d").ap()
    selb_d = nc.inline_tensor(np.ascontiguousarray(sel2_np[1:2]), "selb_d").ap()

    base = (nc.sbuf_base + 63) // 64 * 64
    cur = [base]

    def sb(name, shape, dtype):
        nbytes = int(np.prod(shape[1:])) * mybir.dt.size(dtype)
        at = cur[0]
        cur[0] = at + (nbytes + 31) // 32 * 32
        return nc.alloc_sbuf_tensor_at(name, list(shape), dtype, offset=at)

    RESID = sb("RESID", [128, NT * D], F32)
    XN = sb("XN", [128, 16384], BF16)
    RX = sb("RX", [128, 16384], BF16)
    RY = sb("RY", [128, 16384], BF16)
    SLAB = sb("SLAB", [128, 8192], BF16)
    LAND = sb("LAND", [128, 3584], F32)
    SCRF = sb("SCRF", [128, 2048], F32)
    SCRB = sb("SCRB", [128, 2048], BF16)

    identf = sb("identf", [128, 128], F32)
    lstrict = sb("lstrict", [128, 128], BF16)
    auxsb = sb("auxsb", [128, 8], BF16)
    crow = sb("crow", [1, 128], BF16)
    ones2 = sb("ones2", [128, 2], BF16)
    sel2sb = sb("sel2sb", [2, 128], BF16)
    slni = sb("slni", [128, 16], F32)
    sinv = sb("sinv", [128, 16], F32)
    s1c = sb("s1c", [128, 32], F32)
    s2c = sb("s2c", [128, 32], F32)
    hs1c = sb("hs1c", [128, 32], F32)
    hs2c = sb("hs2c", [128, 32], F32)
    s3c = sb("s3c", [128, 8], F32)
    bqc = sb("bqc", [128, 8], F32)
    bkc = sb("bkc", [128, 8], F32)
    b1c = sb("b1c", [128, 8], F32)
    b2c = sb("b2c", [128, 4], F32)
    b3c = sb("b3c", [128, 2], F32)
    b4c = sb("b4c", [128, 1], F32)
    arc = sb("arc", [128, 1], F32)
    flagc = sb("flagc", [1, 1], F32)
    rcol = sb("rcol", [128, 16], F32)
    mrcol = sb("mrcol", [128, 16], F32)
    stile = sb("stile", [128, 256], F32)
    gfcolb = sb("gfcolb", [128, 8], BF16)
    gfc = sb("gfc", [128, 1], F32)
    acol = sb("acol", [128, 1], F32)
    bcol = sb("bcol", [128, 1], F32)
    ccols = sb("ccols", [128, 2], F32)
    ncols_ = sb("ncols_", [128, 2], F32)
    qkn = sb("qkn", [1, 64], F32)
    selb = sb("selb", [1, 128], BF16)
    negmb = sb("negmb", [1, NH], BF16)
    negm = sb("negm", [128, NH], F32)
    cn4b = sb("cn4b", [128, 2], BF16)
    gfb = sb("gfb", [1, 2], BF16)
    sigt = sb("sigt", [128, 16], F32)
    w2t = sb("w2t", [128, 16], F32)
    assert cur[0] <= nc.sbuf_top, (cur[0], nc.sbuf_top)

    tfcol = sigt[:, 0:1]
    gfcol = negm[:, 0:8]

    scr_silu0 = SCRF[:, 0:512]
    scr_silu1 = SCRF[:, 512:1024]
    scr_pp = SCRF[:, 1024:2048]
    scr_cast = SCRB[:, 0:1024]
    scr_sq = SCRB[:, 1024:2048]
    lnsq = LAND[:, 0:1024]
    tots = LAND[0:1, 0:1024]
    carry = LAND[0:1, 1024:2048]
    carryscr = LAND[0:1, 2048:3072]
    carrybf = SLAB[0:1, 0:2048]
    gfrow = SCRF[0:1, 0:512]
    gfrow2 = SCRF[0:1, 512:1024]
    zrowA = SCRF[0:1, 0:512]
    zrowB = SCRF[0:1, 512:1024]
    zinvA = SCRF[0:1, 1024:1536]
    zinvB = SCRF[0:1, 1536:2048]
    zbf = XN[0:1, 14336:16384]
    nrow = qkn[0:1, 16:32]
    qrow = qkn[0:1, 0:16]

    PS = [nc.alloc_psum_tensor(f"psb{i}", [128, 512], F32) for i in range(8)]

    _wbv = {}

    def stream_sets(sets, consume, land_base=0, land_slots=3, land_cols=1024,
                    pre_guard=None, wb=None):
        mm_entry = pl.c("mm")
        wc_entry = pl.c("wc")
        g = [0]
        castv = []
        mmv = {}
        wbv = {}

        def load_set(si):
            mm_guard = mmv.get(si - 2, mm_entry if pre_guard is None
                               else pre_guard)
            for j, (src, ncols, dst) in enumerate(sets[si]):
                slot = g[0] % land_slots
                qq = ("sy", "gp")[g[0] % 2]
                pl.wait(qq, "wc", max(wc_entry,
                                      pl.c("wc") - land_slots + 1))
                la = LAND[:, land_base + slot * land_cols:
                          land_base + slot * land_cols + ncols]
                dv = pl.dma(qq, la, src, sem="s%d" % slot)
                pl.wait("ve", "s%d" % slot, dv)
                if j == 0 and mm_guard > 0:
                    pl.wait("ve", "mm", mm_guard)
                if wb is not None and (si - 2, j) in wbv:
                    pl.wait("ve", "sy", wbv[(si - 2, j)])
                cv = pl.op("ve", lambda e, la=la, dst=dst:
                           e.tensor_copy(dst, la), sem="wc")
                if wb is not None:
                    pl.wait("sy", "wc", cv)
                    wbv[(si, j)] = pl.dma("sy", wb[si][j], dst)
                g[0] += 1

        nsets = len(sets)
        for si in range(min(2, nsets)):
            load_set(si)
            castv.append(pl.c("wc"))
        for si in range(nsets):
            pl.wait("mm", "wc", castv[si])
            consume(si)
            mmv[si] = pl.c("mm")
            if si + 2 < nsets:
                load_set(si + 2)
                castv.append(pl.c("wc"))

    def stream_sets_direct(sets, consume, depth=2):
        mm_entry = pl.c("mm")
        mmv = {}
        gv = {}

        def load_set(si):
            guard = mmv.get(si - depth, mm_entry)
            last = {"sy": 0, "gp": 0}
            seen = set()
            for j, (src, ncols, dst) in enumerate(sets[si]):
                qq = ("sy", "gp")[(si + j) % 2]
                if qq not in seen and guard > 0:
                    pl.wait(qq, "mm", guard)
                    seen.add(qq)
                last[qq] = pl.dma(qq, dst, src)
            gv[si] = last

        nsets = len(sets)
        for si in range(min(depth, nsets)):
            load_set(si)
        for si in range(nsets):
            if gv[si]["sy"]:
                pl.wait("mm", "sy", gv[si]["sy"])
            if gv[si]["gp"]:
                pl.wait("mm", "gp", gv[si]["gp"])
            consume(si)
            mmv[si] = pl.c("mm")
            if si + depth < nsets:
                load_set(si + depth)

    def slabdst(si, off, ncols):
        b = (si % 2) * 4096
        return SLAB[:, b + off:b + off + ncols]

    for t_, d_ in ((identf, identf_d), (lstrict, lstrict_d),
                   (auxsb, aux_d), (crow, crow_d), (ones2, ones2_d),
                   (sel2sb, sel2_d),
                   (selb, selb_d),
                   (slni, DI["neglni"]), (sinv, DI["invtm"]),
                   (s1c, DI["s1pp"]), (s2c, DI["s2pp"]), (hs1c, DI["hs1pp"]),
                   (hs2c, DI["hs2pp"]), (s3c, DI["s3pp"]), (bqc, DI["bqpp"]),
                   (bkc, DI["bkpp"]), (b1c, DI["b1pp"]), (b2c, DI["b2pp"]),
                   (b3c, DI["b3pp"]), (b4c, DI["b4ex"]), (arc, DI["arex"]),
                   (flagc, DI["cflag"])):
        pl.dma("gp", t_[:], d_)
    for tile in range(NT):
        pl.dma("sy", RESID[:, tile * D:(tile + 1) * D],
               DI["x0"][tile * 128:(tile + 1) * 128, :])
    pl.barrier()

    ST = 1536

    def plan_ln(tile):
        rt = RESID[:, tile * D:(tile + 1) * D]
        pl.op("ve", lambda e, rt=rt: e.tensor_tensor(lnsq, rt, rt, ALU.mult))
        pl.op("ve", lambda e: e.reduce_sum(SCRF[:, ST:ST + 1], lnsq, axis=AXX))
        pl.op("ve", lambda e, rt=rt: e.reduce_sum(SCRF[:, ST + 1:ST + 2], rt,
                                                  axis=AXX))
        pl.op("ve", lambda e: e.tensor_scalar(
            SCRF[:, ST + 2:ST + 3], SCRF[:, ST + 1:ST + 2], 1.0 / D, None,
            ALU.mult))
        pl.op("ve", lambda e: e.tensor_tensor(
            SCRF[:, ST + 3:ST + 4], SCRF[:, ST + 2:ST + 3],
            SCRF[:, ST + 2:ST + 3], ALU.mult))
        pl.op("ve", lambda e: e.tensor_scalar(
            SCRF[:, ST + 4:ST + 5], SCRF[:, ST:ST + 1], 1.0 / D, 1e-5,
            ALU.mult, ALU.add))
        v = pl.op("ve", lambda e: e.tensor_tensor(
            SCRF[:, ST + 4:ST + 5], SCRF[:, ST + 4:ST + 5],
            SCRF[:, ST + 3:ST + 4], ALU.subtract))
        pl.wait("ac", "ve", v)
        v = pl.op("ac", lambda e: e.activation(
            SCRF[:, ST + 5:ST + 6], SCRF[:, ST + 4:ST + 5], ACT.Sqrt))
        pl.wait("ve", "ac", v)
        pl.op("ve", lambda e, tile=tile: e.reciprocal(
            rcol[:, tile:tile + 1], SCRF[:, ST + 5:ST + 6]))
        pl.op("ve", lambda e, tile=tile: e.tensor_tensor(
            mrcol[:, tile:tile + 1], SCRF[:, ST + 2:ST + 3],
            rcol[:, tile:tile + 1], ALU.mult))
        vx = pl.op("ve", lambda e, rt=rt, tile=tile: e.tensor_scalar(
            scr_pp, rt, rcol[:, tile:tile + 1], mrcol[:, tile:tile + 1],
            ALU.mult, ALU.subtract))
        pl.wait("mm", "ve", vx)
        evd = {}
        tvl = 0
        for c in range(8):
            bank = PS[6 + (c % 2)]
            if c >= 2:
                pl.wait("mm", "ac", evd[c - 2])
            tvl = pl.op("mm", lambda e, c=c, bank=bank: e.transpose(
                bank[:, 0:128], scr_pp[:, c * 128:c * 128 + 128], identf[:]))
            pl.wait("ac", "mm", tvl)
            evd[c] = pl.op("ac", lambda e, c=c, tile=tile, bank=bank:
                           e.activation(XN[:, c * 2048 + tile * 128:
                                           c * 2048 + tile * 128 + 128],
                                        bank[:, 0:128], ACT.Identity))
        pl.wait("ve", "mm", tvl)

    for tile in range(NT):
        plan_ln(tile)
    pl.barrier()

    if stage == "x0":
        return _finish(nc, pl, RESID, yout)

    qkv0_sets = []
    for ccn in range(6):
        qkv0_sets.append([(DI["w0qkv"][dc, :, ccn * 512:(ccn + 1) * 512], 512,
                           slabdst(ccn, dc * 512, 512)) for dc in range(8)])

    _q0evac = {}

    def qkv0_consume(ccn):
        half = ccn % 2
        for tile in range(NT):
            key = ccn * NT + tile
            bank = PS[key % 4]
            if key >= 4:
                sname, sval = _q0evac[key - 4]
                pl.wait("mm", sname, sval)
            mv = None
            for dc in range(8):
                f = _mmf(bank, XN[:, dc * 2048 + tile * 128:
                                  dc * 2048 + tile * 128 + 128],
                         slabdst(ccn, dc * 512, 512), dc == 0, dc == 7)
                if dc == 7:
                    mv = pl.op("mm", f)
                else:
                    pl.opn("mm", f)
            sl = slice(tile * D + half * 512, tile * D + half * 512 + 512)
            if ccn in (0, 1):
                pl.wait("ac", "mm", mv)
                ev = pl.op("ac", lambda e, bank=bank, sl=sl:
                           e.copy(RY[:, sl], bank[:]))
                _q0evac[key] = ("ac", ev)
            elif ccn in (2, 3):
                pl.wait("ve", "mm", mv)
                pl.wait("ve", "ac", _q0evac[(ccn - 2) * NT + tile][1])
                pl.op("ve", lambda e, bank=bank, sl=sl: e.tensor_tensor(
                    scr_silu0, RY[:, sl], bank[:], ALU.mult))
                v2 = None
                for hh in range(8):
                    h = (ccn - 2) * 8 + hh
                    v2 = pl.op("ve", lambda e, hh=hh, h=h, tile=tile:
                               e.reduce_sum(
                                   stile[:, tile * 16 + h:tile * 16 + h + 1],
                                   scr_silu0[:, hh * 64:(hh + 1) * 64],
                                   axis=AXX))
                _q0evac[key] = ("ve", v2)
            else:
                pl.wait("ac", "mm", mv)
                ev = pl.op("ac", lambda e, bank=bank, sl=sl:
                           e.copy(RX[:, sl], bank[:]))
                _q0evac[key] = ("ac", ev)
                pl.wait("mm", "ac", ev)
                tb = PS[6] if half == 0 else PS[7]
                f = _mmf(tb[0:1, :], auxsb[:, 0:1], RX[:, sl],
                         tile == 0, tile == NT - 1)
                if tile == NT - 1:
                    pl.op("mm", f)
                else:
                    pl.opn("mm", f)

    stream_sets(qkv0_sets, qkv0_consume)
    pl.barrier()

    pl.op("ve", lambda e: e.tensor_copy(tots[0:1, 0:512], PS[6][0:1, :]))
    v = pl.op("ve", lambda e: e.tensor_copy(tots[0:1, 512:1024], PS[7][0:1, :]))
    pl.wait("gp", "ve", v)
    gv = pl.dma("gp", cc_a_in, tots[:])
    pl.wait("gp", "gp", gv)
    ccv = pl.op("gp", lambda e: e.collective_compute(
        "AllGather", ALU.bypass, replica_groups=RG,
        ins=[cc_a_in.opt()], outs=[cc_a_out.opt()]), sem="cc")
    pl.wait("gp", "cc", ccv)
    gv = pl.dma("gp", scr_pp[0:1, :], cc_a_out[0:1, :])
    pl.wait("ve", "gp", gv)
    pl.op("ve", lambda e: e.tensor_scalar(
        carry[:], scr_pp[0:1, :], flagc[0:1, 0:1], None, ALU.mult))

    def carry_hilo():
        pl.op("ve", lambda e: e.tensor_copy(carrybf[0:1, 0:1024], carry[:]))
        pl.op("ve", lambda e: e.tensor_tensor(
            carryscr, carry[:], carrybf[0:1, 0:1024], ALU.subtract))
        return pl.op("ve", lambda e: e.tensor_copy(carrybf[0:1, 1024:2048],
                                                   carryscr))

    cv = carry_hilo()
    pl.barrier()

    for tile in range(NT):
        pl.wait("ac", "ve")
        av = pl.op("ac", lambda e, tile=tile: e.activation(
            sigt[:], stile[:, tile * 16:tile * 16 + 16], ACT.Sigmoid,
            bias=slni[:, tile:tile + 1], scale=0.125))
        pl.wait("ve", "ac", av)
        pl.op("ve", lambda e: e.tensor_scalar(w2t[:], sigt[:], -1.0, 1.0,
                                              ALU.mult, ALU.add))
        wsig = pl.op("ve", lambda e, tile=tile: e.tensor_scalar(
            w2t[:], w2t[:], sinv[:, tile:tile + 1], None, ALU.mult))
        for half in range(2):
            bank = PS[half]
            sl = slice(tile * D + half * 512, tile * D + half * 512 + 512)
            pl.opn("mm", _mmf(bank, lstrict[:], RX[:, sl], True, False))
            pl.opn("mm", _mmf(bank, crow[:],
                              carrybf[0:1, half * 512:half * 512 + 512],
                              False, False))
            mv = pl.op("mm", _mmf(bank, crow[:],
                                  carrybf[0:1, 1024 + half * 512:
                                          1024 + half * 512 + 512],
                                  False, True))
            pl.wait("ve", "mm", mv)
            pl.wait("ve", "ve", wsig)
            sig_e = sigt[:, half * 8:half * 8 + 8].unsqueeze(2) \
                .broadcast_to([128, 8, 64])
            w2_e = w2t[:, half * 8:half * 8 + 8].unsqueeze(2) \
                .broadcast_to([128, 8, 64])
            pl.op("ve", lambda e, sl=sl, sig_e=sig_e: e.tensor_tensor(
                scr_silu0.rearrange("p (a b) -> p a b", a=8),
                RX[:, sl].rearrange("p (a b) -> p a b", a=8), sig_e, ALU.mult))
            pl.op("ve", lambda e, bank=bank, w2_e=w2_e: e.tensor_tensor(
                scr_silu1.rearrange("p (a b) -> p a b", a=8),
                bank[:].rearrange("p (a b) -> p a b", a=8), w2_e, ALU.mult))
            pl.op("ve", lambda e, sl=sl: e.tensor_tensor(
                RESID[:, sl], RESID[:, sl], scr_silu0, ALU.add))
            v3 = pl.op("ve", lambda e, sl=sl: e.tensor_tensor(
                RESID[:, sl], RESID[:, sl], scr_silu1, ALU.add))
        if tile < NT - 1:
            for half in range(2):
                tb = PS[2 + half]
                sl = slice(tile * D + half * 512, tile * D + half * 512 + 512)
                mv = pl.op("mm", _mmf(tb[0:1, :], auxsb[:, 0:1], RX[:, sl],
                                      True, True))
                pl.wait("ve", "mm", mv)
                pl.op("ve", lambda e, tb=tb, half=half: e.tensor_tensor(
                    carry[0:1, half * 512:half * 512 + 512],
                    carry[0:1, half * 512:half * 512 + 512], tb[0:1, :],
                    ALU.add))
            cv = carry_hilo()
            pl.wait("mm", "ve", cv)
    pl.barrier()

    if stage == "attn0":
        return _finish(nc, pl, RESID, yout)

    def ffn0(tt):
        first = tt == 0
        if not first:
            pl.wait("gp", "sy", _wb_done[0])
        pl.wait("ve", "mm")
        pl.wait("ac", "mm")
        def s1d(si, off, n):
            if first:
                return slabdst(si, off, n)
            b = (si % 4) * 1024
            return SLAB[:, b + off:b + off + n]

        s1_sets = []
        s1_wb = []
        for oc in range(32):
            for wname in ("w0w1", "w0w2"):
                si = len(s1_sets)
                src = DI[wname] if first else SC[wname]
                s1_sets.append([(src[oc, :, 0:1024], 1024,
                                 s1d(si, 0, 1024))])
                s1_wb.append([SC[wname][oc, :, 0:1024]])

        _s1evac = {}

        def s1_consume(si):
            oc, w = si // 2, si % 2
            bank = PS[w * 2 + (oc % 2)]
            if w == 0 and oc >= 2:
                pl.wait("mm", "ve", _s1evac[oc - 2])
            mv = None
            for dc in range(8):
                f = _mmf(bank, s1d(si, dc * 128, 128),
                         XN[:, dc * 2048 + tt * 512:dc * 2048 + tt * 512 + 512],
                         dc == 0, dc == 7)
                if dc == 7:
                    mv = pl.op("mm", f)
                else:
                    pl.opn("mm", f)
            if w == 1:
                b1, b2 = PS[(oc % 2)], PS[2 + (oc % 2)]
                slt = scr_silu0 if oc % 2 == 0 else scr_silu1
                pl.wait("ac", "mm", mv)
                if oc >= 2:
                    pl.wait("ac", "ve", _s1evac[oc - 2])
                av = pl.op("ac", lambda e, b1=b1, oc=oc, slt=slt: e.activation(
                    slt, b1[:], ACT.Silu, scale=s1c[:, oc:oc + 1]))
                pl.wait("ve", "ac", av)
                vv = pl.op("ve", lambda e, b2=b2, oc=oc, slt=slt:
                           e.scalar_tensor_tensor(
                               RX[:, oc * 512:oc * 512 + 512], b2[:],
                               s2c[:, oc:oc + 1], slt, ALU.mult, ALU.mult))
                _s1evac[oc] = vv

        if first:
            stream_sets(s1_sets, s1_consume, wb=s1_wb)
        else:
            stream_sets_direct(s1_sets, s1_consume, depth=4)
        pl.wait("mm", "ve")

        def hstage(wname, srcbuf, dstbuf, scol):
            src = DI[wname] if first else SC[wname]
            sets = []
            wbl = []
            for oc in range(32):
                sets.append([(src[oc, :, gpc * 1024:(gpc + 1) * 1024],
                              1024, slabdst(oc, gpc * 1024, 1024))
                             for gpc in range(4)])
                wbl.append([SC[wname][oc, :, gpc * 1024:(gpc + 1) * 1024]
                            for gpc in range(4)])
            _hevac = {}

            def consume(oc):
                bank = PS[oc % 4]
                if oc >= 4:
                    pl.wait("mm", "ac", _hevac[oc - 4])
                mv = None
                for dc in range(32):
                    f = _mmf(bank, slabdst(oc, dc * 128, 128),
                             srcbuf[:, dc * 512:dc * 512 + 512],
                             dc == 0, dc == 31)
                    if dc == 31:
                        mv = pl.op("mm", f)
                    else:
                        pl.opn("mm", f)
                pl.wait("ac", "mm", mv)
                av = pl.op("ac", lambda e, bank=bank, oc=oc: e.activation(
                    dstbuf[:, oc * 512:oc * 512 + 512], bank[:], ACT.Gelu,
                    scale=scol[:, oc:oc + 1]))
                _hevac[oc] = av

            if first:
                stream_sets(sets, consume, wb=wbl)
            else:
                hsets = []
                for oc in range(32):
                    for hf in range(2):
                        si = oc * 2 + hf
                        hsets.append(
                            [(src[oc, :, (hf * 2 + q) * 1024:
                                  (hf * 2 + q + 1) * 1024], 1024,
                              SLAB[:, (si % 4) * 2048 + q * 1024:
                                   (si % 4) * 2048 + q * 1024 + 1024])
                             for q in range(2)])

                def hconsume(si):
                    oc, hf = si // 2, si % 2
                    bank = PS[oc % 4]
                    if hf == 0 and oc >= 4:
                        pl.wait("mm", "ac", _hevac[oc - 4])
                    base = (si % 4) * 2048
                    mv = None
                    for dc in range(16):
                        gdc = hf * 16 + dc
                        f = _mmf(bank, SLAB[:, base + dc * 128:
                                            base + dc * 128 + 128],
                                 srcbuf[:, gdc * 512:gdc * 512 + 512],
                                 gdc == 0, gdc == 31)
                        if dc == 15:
                            mv = pl.op("mm", f)
                        else:
                            pl.opn("mm", f)
                    if hf == 1:
                        pl.wait("ac", "mm", mv)
                        av = pl.op("ac", lambda e, bank=bank, oc=oc:
                                   e.activation(
                                       dstbuf[:, oc * 512:oc * 512 + 512],
                                       bank[:], ACT.Gelu,
                                       scale=scol[:, oc:oc + 1]))
                        _hevac[oc] = av

                stream_sets_direct(hsets, hconsume, depth=4)
            pl.wait("mm", "ac")

        hstage("w0h1", RX, RY, hs1c)
        hstage("w0h2", RY, RX, hs2c)

        w3src = DI["w0w3"] if first else SC["w0w3"]
        w3_sets = []
        w3_wb = []
        for ec in range(8):
            w3_sets.append([(w3src[ec, :, gpc * 1024:(gpc + 1) * 1024],
                             1024, slabdst(ec, gpc * 1024, 1024))
                            for gpc in range(4)])
            w3_wb.append([SC["w0w3"][ec, :, gpc * 1024:(gpc + 1) * 1024]
                          for gpc in range(4)])

        def w3_consume(ec):
            bank = PS[4 + ec % 2]
            mv = None
            for dc in range(32):
                f = _mmf(bank, slabdst(ec, dc * 128, 128),
                         RX[:, dc * 512:dc * 512 + 512], dc == 0, dc == 31)
                if dc == 31:
                    mv = pl.op("mm", f)
                else:
                    pl.opn("mm", f)
            pl.wait("ac", "mm", mv)
            if ec >= 1:
                pl.wait("ac", "mm")
            av = pl.op("ac", lambda e, bank=bank, ec=ec: e.activation(
                scr_silu1, bank[:], ACT.Identity, scale=s3c[:, ec:ec + 1]))
            pl.wait("mm", "ac", av)
            for ts_ in range(4):
                tv = pl.op("mm", lambda e, ts_=ts_: e.transpose(
                    PS[6][:, 0:128], scr_silu1[:, ts_ * 128:ts_ * 128 + 128],
                    identf[:]))
                pl.wait("ve", "mm", tv)
                gt = tt * 4 + ts_
                vv = pl.op("ve", lambda e, gt=gt, ec=ec: e.tensor_tensor(
                    RESID[:, gt * D + ec * 128:gt * D + ec * 128 + 128],
                    RESID[:, gt * D + ec * 128:gt * D + ec * 128 + 128],
                    PS[6][:, 0:128], ALU.add))
                pl.wait("mm", "ve", vv)

        if first:
            stream_sets(w3_sets, w3_consume, wb=w3_wb)
        else:
            w3h = []
            for ec in range(8):
                for hf in range(2):
                    si = ec * 2 + hf
                    w3h.append(
                        [(w3src[ec, :, (hf * 2 + q) * 1024:
                                (hf * 2 + q + 1) * 1024], 1024,
                          SLAB[:, (si % 4) * 2048 + q * 1024:
                               (si % 4) * 2048 + q * 1024 + 1024])
                         for q in range(2)])

            def w3h_consume(si):
                ec, hf = si // 2, si % 2
                bank = PS[4 + ec % 2]
                base = (si % 4) * 2048
                mv = None
                for dc in range(16):
                    gdc = hf * 16 + dc
                    f = _mmf(bank, SLAB[:, base + dc * 128:
                                        base + dc * 128 + 128],
                             RX[:, gdc * 512:gdc * 512 + 512],
                             gdc == 0, gdc == 31)
                    if dc == 15:
                        mv = pl.op("mm", f)
                    else:
                        pl.opn("mm", f)
                if hf == 0:
                    return
                pl.wait("ac", "mm", mv)
                if ec >= 1:
                    pl.wait("ac", "mm")
                av = pl.op("ac", lambda e, bank=bank, ec=ec: e.activation(
                    scr_silu1, bank[:], ACT.Identity, scale=s3c[:, ec:ec + 1]))
                pl.wait("mm", "ac", av)
                for ts_ in range(4):
                    tv = pl.op("mm", lambda e, ts_=ts_: e.transpose(
                        PS[6][:, 0:128],
                        scr_silu1[:, ts_ * 128:ts_ * 128 + 128], identf[:]))
                    pl.wait("ve", "mm", tv)
                    gt = tt * 4 + ts_
                    vv = pl.op("ve", lambda e, gt=gt, ec=ec: e.tensor_tensor(
                        RESID[:, gt * D + ec * 128:gt * D + ec * 128 + 128],
                        RESID[:, gt * D + ec * 128:gt * D + ec * 128 + 128],
                        PS[6][:, 0:128], ALU.add))
                    pl.wait("mm", "ve", vv)

            stream_sets_direct(w3h, w3h_consume, depth=4)

    _wb_done = [0]
    for tt in range(4):
        ffn0(tt)
        if tt == 0:
            _wb_done[0] = pl.c("sy")
    pl.barrier()

    if stage == "ffn0":
        return _finish(nc, pl, RESID, yout)

    for tile in range(NT):
        plan_ln(tile)
    pl.wait("ve", "ac")
    for c in range(8):
        v = pl.op("ve", lambda e, c=c: e.reduce_sum(
            gfcol[:, c:c + 1], XN[:, c * 2048:(c + 1) * 2048], axis=AXX))
    pl.wait("gp", "ve", v)
    gv = pl.dma("gp", cc_g_in, gfcol)
    pl.wait("gp", "gp", gv)
    ccg = pl.op("gp", lambda e: e.collective_compute(
        "AllGather", ALU.bypass, replica_groups=RG,
        ins=[cc_g_in.opt()], outs=[cc_g_out.opt()]), sem="cc")
    pl.barrier()

    def qk_sets(wname):
        return [[(DI[wname][oc, :, 0:1024], 1024, slabdst(oc, 0, 1024))]
                for oc in range(8)]

    _qevac = {}
    _qnv = [0]

    def q_consume(oc):
        for tc in range(4):
            key = oc * 4 + tc
            bank = PS[key % 4]
            if key >= 4:
                pl.wait("mm", "ac", _qevac[key - 4])
            mv = None
            for dc in range(8):
                f = _mmf(bank, slabdst(oc, dc * 128, 128),
                         XN[:, dc * 2048 + tc * 512:dc * 2048 + tc * 512 + 512],
                         dc == 0, dc == 7)
                if dc == 7:
                    mv = pl.op("mm", f)
                else:
                    pl.opn("mm", f)
            pl.wait("ac", "mm", mv)
            ev = pl.op("ac", lambda e, bank=bank, oc=oc, tc=tc: e.activation(
                RX[:, oc * 2048 + tc * 512:oc * 2048 + tc * 512 + 512],
                bank[:], ACT.Identity, bias=bqc[:, oc:oc + 1]))
            _qevac[key] = ev
            pl.wait("ve", "ac", ev)
            if _qnv[0]:
                pl.wait("ve", "mm", _qnv[0])
            sv = pl.op("ve", lambda e, oc=oc, tc=tc: e.tensor_tensor(
                scr_sq[:, 0:512],
                RX[:, oc * 2048 + tc * 512:oc * 2048 + tc * 512 + 512],
                RX[:, oc * 2048 + tc * 512:oc * 2048 + tc * 512 + 512],
                ALU.mult))
            pl.wait("mm", "ve", sv)
            pl.opn("mm", _mmf(PS[4][0:1, :], auxsb[:, 1:2],
                              scr_sq[:, 0:512], tc == 0, tc == 3))
            nv = pl.op("mm", _mmf(PS[5][0:1, :], auxsb[:, 2:3],
                                  scr_sq[:, 0:512], tc == 0, tc == 3))
            _qnv[0] = nv
            if tc == 3:
                pl.wait("ve", "mm", nv)
                pl.op("ve", lambda e, oc=oc: e.reduce_max(
                    qrow[0:1, 2 * oc:2 * oc + 1], PS[4][0:1, :], axis=AXX))
                rv = pl.op("ve", lambda e, oc=oc: e.reduce_max(
                    qrow[0:1, 2 * oc + 1:2 * oc + 2], PS[5][0:1, :], axis=AXX))
                pl.wait("mm", "ve", rv)

    _kdma, _ksq = {}, {}
    _knv = [0]

    def k_consume(oc):
        for tc in range(4):
            key = oc * 4 + tc
            bank = PS[key % 4]
            if key >= 4:
                pl.wait("mm", "d" + str(key % 2), _kdma[key - 4])
            mv = None
            for dc in range(8):
                f = _mmf(bank, slabdst(oc, dc * 128, 128),
                         XN[:, dc * 2048 + tc * 512:dc * 2048 + tc * 512 + 512],
                         dc == 0, dc == 7)
                if dc == 7:
                    mv = pl.op("mm", f)
                else:
                    pl.opn("mm", f)
            slot = scr_cast[:, (key % 2) * 512:(key % 2) * 512 + 512]
            pl.wait("ac", "mm", mv)
            if key >= 2:
                pl.wait("ac", "d" + str(key % 2), _kdma[key - 2])
                pl.wait("ac", "ve", _ksq[key - 2])
            ev = pl.op("ac", lambda e, bank=bank, oc=oc, slot=slot:
                       e.activation(slot, bank[:], ACT.Identity,
                                    bias=bkc[:, oc:oc + 1]))
            pl.wait("sy", "ac", ev)
            _kdma[key] = pl.dma("sy", cc_k_in[oc * 128:oc * 128 + 128,
                                              tc * 512:tc * 512 + 512], slot,
                                sem="d" + str(key % 2))
            pl.wait("ve", "ac", ev)
            if _knv[0]:
                pl.wait("ve", "mm", _knv[0])
            sv = pl.op("ve", lambda e, slot=slot: e.tensor_tensor(
                scr_sq[:, 512:1024], slot, slot, ALU.mult))
            _ksq[key] = sv
            pl.wait("mm", "ve", sv)
            pl.opn("mm", _mmf(PS[6][0:1, :], auxsb[:, 1:2],
                              scr_sq[:, 512:1024], tc == 0, tc == 3))
            nv = pl.op("mm", _mmf(PS[7][0:1, :], auxsb[:, 2:3],
                                  scr_sq[:, 512:1024], tc == 0, tc == 3))
            _knv[0] = nv
            if tc == 3:
                pl.wait("ve", "mm", nv)
                pl.op("ve", lambda e, oc=oc: e.reduce_max(
                    nrow[0:1, 2 * oc:2 * oc + 1], PS[6][0:1, :], axis=AXX))
                rv = pl.op("ve", lambda e, oc=oc: e.reduce_max(
                    nrow[0:1, 2 * oc + 1:2 * oc + 2], PS[7][0:1, :], axis=AXX))
                pl.wait("mm", "ve", rv)

    stream_sets(qk_sets("w1q"), q_consume)
    stream_sets(qk_sets("w1k"), k_consume)
    pl.barrier()

    bvex_sb = scr_pp
    bv_v = pl.dma("gp", bvex_sb, DI["bvex"])
    pl.wait("ve", "gp", bv_v)
    vsets = [[(DI["w1v"][dc, :, 0:1024], 1024,
               SLAB[:, dc * 1024:dc * 1024 + 1024]) for dc in range(8)]]
    _vdma = {}

    def v_consume(_si):
        for tile in range(NT):
            for oh in range(2):
                key = tile * 2 + oh
                bank = PS[key % 4]
                if key >= 4:
                    pl.wait("mm", "d" + str(key % 2), _vdma[key - 4])
                mv = None
                for dc in range(8):
                    f = _mmf(bank, XN[:, dc * 2048 + tile * 128:
                                      dc * 2048 + tile * 128 + 128],
                             SLAB[:, dc * 1024 + oh * 512:
                                  dc * 1024 + oh * 512 + 512],
                             dc == 0, dc == 7)
                    if dc == 7:
                        mv = pl.op("mm", f)
                    else:
                        pl.opn("mm", f)
                slot = scr_cast[:, (key % 2) * 512:(key % 2) * 512 + 512]
                pl.wait("ve", "mm", mv)
                if key >= 2:
                    pl.wait("ve", "d" + str(key % 2), _vdma[key - 2])
                ev = pl.op("ve", lambda e, bank=bank, oh=oh, slot=slot:
                           e.tensor_tensor(slot, bank[:],
                                           bvex_sb[:, oh * 512:oh * 512 + 512],
                                           ALU.add))
                pl.wait("sy", "ve", ev)
                _vdma[key] = pl.dma(
                    "sy", ccv_in[tile * 128:tile * 128 + 128,
                                  oh * 512:oh * 512 + 512], slot,
                    sem="d" + str(key % 2))

    stream_sets(vsets, v_consume)
    pl.barrier()

    ccK = pl.op("gp", lambda e: e.collective_compute(
        "AllGather", ALU.bypass, replica_groups=RG,
        ins=[cc_k_in.opt()], outs=[cc_k_out.opt()]), sem="cc")
    ccV = pl.op("gp", lambda e: e.collective_compute(
        "AllGather", ALU.bypass, replica_groups=RG,
        ins=[ccv_in.opt()], outs=[ccv_out.opt()]), sem="cc")
    pl.wait("gp", "ve", pl.c("ve"))
    gv = pl.dma("gp", cc_n_in, nrow)
    pl.wait("gp", "gp", gv)
    ccN = pl.op("gp", lambda e: e.collective_compute(
        "AllGather", ALU.bypass, replica_groups=RG,
        ins=[cc_n_in.opt()], outs=[cc_n_out.opt()]), sem="cc")

    if stage == "qkv1":
        return _finish(nc, pl, RESID, yout)

    pl.wait("gp", "cc", ccg)
    pl.dma("gp", gfcol, cc_g_out[0:128, :])
    gv = pl.dma("gp", scr_pp[:, 0:8], cc_g_out[128:256, :])
    pl.wait("ve", "gp", gv)
    pl.op("ve", lambda e: e.tensor_tensor(
        gfcol, gfcol, scr_pp[:, 0:8], ALU.add))
    v = pl.op("ve", lambda e: e.tensor_scalar(
        gfcol, gfcol, 1.0 / SEQ, None, ALU.mult))
    pl.wait("mm", "ve", v)

    def row_to_cols(src_row, n, dstcol):
        for c in range(n):
            tv = pl.op("mm", lambda e, c=c, src_row=src_row: e.transpose(
                PS[0][:, 0:1], src_row[0:1, c * 128:c * 128 + 128],
                identf[0:1, 0:1]))
            pl.wait("ve", "mm", tv)
            vv = pl.op("ve", lambda e, c=c, dstcol=dstcol: e.tensor_copy(
                dstcol[:, c:c + 1], PS[0][:, 0:1]))
            pl.wait("mm", "ve", vv)
        return vv

    v = pl.op("ve", lambda e: e.tensor_copy(gfcolb[:], gfcol[:]))
    cvv = None
    for nm, ncol, base_ in (("gc1", 512, 0), ("gc2", 256, 4096),
                            ("gc3", 1, 6144)):
        for dc in range(DI[nm].shape[0]):
            pl.wait("gp", "ve")
            lv = pl.dma("gp", LAND[:, (dc % 2) * 1024:(dc % 2) * 1024 + ncol],
                        DI[nm][dc, :, :])
            pl.wait("ve", "gp", lv)
            cvv = pl.op("ve", lambda e, dc=dc, ncol=ncol, base_=base_:
                        e.tensor_copy(
                            SLAB[:, base_ + dc * ncol:base_ + (dc + 1) * ncol],
                            LAND[:, (dc % 2) * 1024:(dc % 2) * 1024 + ncol]))
    pl.wait("mm", "ve", cvv)

    def gf_stage(nsl, rhs_of, width, brow_d, brow_n, actf):
        mv = None
        for dc in range(nsl):
            f = _mmf(PS[1][0:1, 0:width], gfcolb[:, dc:dc + 1], rhs_of(dc),
                     dc == 0, dc == nsl - 1)
            if dc == nsl - 1:
                mv = pl.op("mm", f)
            else:
                pl.opn("mm", f)
        pl.wait("gp", "ve")
        bv = pl.dma("gp", gfrow2[0:1, 0:brow_n], brow_d)
        pl.wait("ve", "mm", mv)
        pl.wait("ve", "gp", bv)
        v = pl.op("ve", lambda e, width=width: e.tensor_tensor(
            gfrow[0:1, 0:width], PS[1][0:1, 0:width], gfrow2[0:1, 0:width],
            ALU.add))
        pl.wait("ac", "ve", v)
        av = pl.op("ac", lambda e, width=width, actf=actf: e.activation(
            gfrow[0:1, 0:width], gfrow[0:1, 0:width], actf))
        pl.wait("mm", "ac", av)
        pl.wait("ve", "ac", av)

    gf_stage(8, lambda dc: SLAB[:, dc * 512:dc * 512 + 512], 512,
             DI["gb1"], 512, ACT.Gelu)
    v = row_to_cols(gfrow, 4, gfcol)
    pl.wait("ve", "ve", v)
    v = pl.op("ve", lambda e: e.tensor_copy(gfcolb[:, 0:4], gfcol[:, 0:4]))
    pl.wait("mm", "ve", v)
    gf_stage(4, lambda dc: SLAB[:, 4096 + dc * 256:4096 + dc * 256 + 256], 256,
             DI["gb2"], 256, ACT.Gelu)
    v = row_to_cols(gfrow, 2, gfcol)
    pl.wait("ve", "ve", v)
    v = pl.op("ve", lambda e: e.tensor_copy(gfcolb[:, 0:2], gfcol[:, 0:2]))
    pl.wait("mm", "ve", v)
    gf_stage(2, lambda dc: SLAB[:, 6144 + dc:6144 + dc + 1], 1,
             DI["gb3"], 1, ACT.Sigmoid)
    pl.op("ve", lambda e: e.tensor_copy(gfb[0:1, 0:1], gfrow[0:1, 0:1]))
    pl.op("ve", lambda e: e.tensor_tensor(
        gfrow[0:1, 1:2], gfrow[0:1, 0:1], gfb[0:1, 0:1], ALU.subtract))
    v = pl.op("ve", lambda e: e.tensor_copy(gfb[0:1, 1:2], gfrow[0:1, 1:2]))
    pl.wait("mm", "ve", v)
    pl.opn("mm", _mmf(PS[1][:, 0:1], crow[:], gfb[0:1, 0:1], True, False))
    mv = pl.op("mm", _mmf(PS[1][:, 0:1], crow[:], gfb[0:1, 1:2], False, True))
    pl.wait("ve", "mm", mv)
    pl.op("ve", lambda e: e.tensor_copy(gfc[:], PS[1][:, 0:1]))
    pl.op("ve", lambda e: e.tensor_tensor(acol[:], arc[:], gfc[:], ALU.mult))
    pl.op("ve", lambda e: e.tensor_scalar(acol[:], acol[:], 1.4, None,
                                          ALU.mult))
    pl.op("ve", lambda e: e.tensor_tensor(acol[:], acol[:], arc[:],
                                          ALU.subtract))
    pl.op("ve", lambda e: e.tensor_scalar(acol[:], acol[:], 1.0, 0.1,
                                          ALU.mult, ALU.add))
    pl.op("ve", lambda e: e.tensor_scalar(bcol[:], arc[:], 0.6, None,
                                          ALU.mult))
    for dc in range(2):
        pl.wait("gp", "ve")
        lv = pl.dma("gp", LAND[:, 0:1], DI["cn4"][dc, :, :])
        pl.wait("ve", "gp", lv)
        cvv = pl.op("ve", lambda e, dc=dc: e.tensor_copy(cn4b[:, dc:dc + 1],
                                                         LAND[:, 0:1]))
        pl.wait("gp", "ve", cvv)
    pl.barrier()

    cn_desc = []
    for oc in range(8):
        for hf in range(2):
            cn_desc.append(("cn1", oc, hf, (oc * 2 + hf) * 512))
    for oc in range(4):
        for hf in range(2):
            cn_desc.append(("cn2", oc, hf, 8192 + (oc * 2 + hf) * 512))
    for oc in range(2):
        cn_desc.append(("cn3", oc, 0, 12288 + oc * 512))
    cn_sets = [[(DI[nm][oc, :, hf * 512:hf * 512 + 512], 512,
                 XN[:, xo:xo + 512])] for (nm, oc, hf, xo) in cn_desc]

    def cn_cache_consume(si):
        pl.op("mm", lambda e: e.nop())

    stream_sets(cn_sets, cn_cache_consume, land_base=2048, land_slots=2,
                land_cols=512)
    pl.barrier()

    def ffn1_eighth(egt):
        sreg = LAND[:, 0:2048]
        pl.wait("ve", "mm")
        v = None
        for lt in range(2):
            tile = egt * 2 + lt
            v = pl.op("ve", lambda e, tile=tile, lt=lt: e.tensor_scalar(
                sreg[:, lt * D:lt * D + D],
                RESID[:, tile * D:tile * D + D],
                rcol[:, tile:tile + 1], mrcol[:, tile:tile + 1],
                ALU.mult, ALU.subtract))
        pl.wait("ve", "ve", v)
        for cyc in range(3):
            pl.wait("mm", "ve")
            evd = {}
            for lt in range(2):
                for c in range(8):
                    k = lt * 8 + c
                    bank = PS[6 + (k % 2)]
                    if k >= 2:
                        pl.wait("mm", "ac", evd[k - 2])
                    tvl = pl.op("mm", lambda e, lt=lt, c=c, bank=bank:
                                e.transpose(bank[:, 0:128],
                                            sreg[:, lt * D + c * 128:
                                                 lt * D + c * 128 + 128],
                                            identf[:]))
                    pl.wait("ac", "mm", tvl)
                    evd[k] = pl.op("ac", lambda e, lt=lt, c=c, bank=bank:
                                   e.activation(
                                       SLAB[:, c * 256 + lt * 128:
                                            c * 256 + lt * 128 + 128],
                                       bank[:, 0:128], ACT.Identity))
            pl.wait("mm", "ac")

            _cne = {}
            _stage_last = {}
            desc2 = [("cn1", 8, 2, 8, 0, 2048, b1c),
                     ("cn2", 4, 2, 8, 2048, 4096, b2c),
                     ("cn3", 2, 1, 4, 4096, 5120, b3c)]
            for (nm, noc, nh, nd, srcb, dstb, bias) in desc2:
                for oc in range(noc):
                    bank = PS[oc % 4]
                    gk = ("bank", nm, oc % 4)
                    if gk in _cne:
                        pl.wait("mm", "ac", _cne[gk])
                    if oc == 0 and nm == "cn2":
                        pl.wait("mm", "ac", _stage_last["cn1"])
                    if oc == 0 and nm == "cn3":
                        pl.wait("mm", "ac", _stage_last["cn2"])
                    if nm == "cn1":
                        xbase = (oc * 2) * 512
                    elif nm == "cn2":
                        xbase = 8192 + (oc * 2) * 512
                    else:
                        xbase = 12288 + oc * 512
                    mv = None
                    for dc in range(nd):
                        f = _mmf(bank[:, 0:256],
                                 XN[:, xbase + dc * 128:xbase + dc * 128 + 128],
                                 SLAB[:, srcb + dc * 256:srcb + dc * 256 + 256],
                                 dc == 0, dc == nd - 1)
                        if dc == nd - 1:
                            mv = pl.op("mm", f)
                        else:
                            pl.opn("mm", f)
                    pl.wait("ac", "mm", mv)
                    av = pl.op("ac", lambda e, bank=bank, oc=oc, dstb=dstb,
                               bias=bias: e.activation(
                                   SLAB[:, dstb + oc * 256:
                                        dstb + oc * 256 + 256],
                                   bank[:, 0:256], ACT.Gelu,
                                   bias=bias[:, oc:oc + 1]))
                    _cne[gk] = av
                    _stage_last[nm] = av
            pl.wait("mm", "ac")
            for lt in range(2):
                bank = PS[4 + lt % 2]
                mv = None
                for dc in range(2):
                    f = _mmf(bank[:, 0:1],
                             SLAB[:, 5120 + dc * 256 + lt * 128:
                                  5120 + dc * 256 + lt * 128 + 128],
                             cn4b[:, dc:dc + 1], dc == 0, dc == 1)
                    if dc == 1:
                        mv = pl.op("mm", f)
                    else:
                        pl.opn("mm", f)
                pl.wait("ac", "mm", mv)
                av = pl.op("ac", lambda e, bank=bank: e.activation(
                    tfcol[:], bank[:, 0:1], ACT.Sigmoid, bias=b4c[:, 0:1]))
                pl.wait("ve", "ac", av)
                pl.op("ve", lambda e, lt=lt: e.tensor_scalar(
                    ccols[:, lt:lt + 1], tfcol[:], bcol[:, 0:1], acol[:, 0:1],
                    ALU.mult, ALU.add))
                pl.op("ve", lambda e, lt=lt: e.tensor_scalar(
                    ncols_[:, lt:lt + 1], ccols[:, lt:lt + 1], -1.0, None,
                    ALU.mult))
                st = sreg[:, lt * D:lt * D + D]
                pl.op("ve", lambda e, st=st: e.tensor_copy(
                    scr_silu0[:, 0:256], st[:, 0:256]))
                pl.op("ve", lambda e, st=st: e.tensor_copy(
                    scr_silu0[:, 256:512], st[:, 256:512]))
                pl.op("ve", lambda e, st=st, lt=lt: e.scalar_tensor_tensor(
                    st[:, 0:256], st[:, 512:768], ccols[:, lt:lt + 1],
                    st[:, 0:256], ALU.mult, ALU.add))
                pl.op("ve", lambda e, st=st, lt=lt: e.scalar_tensor_tensor(
                    st[:, 256:512], st[:, 768:1024], ncols_[:, lt:lt + 1],
                    st[:, 256:512], ALU.mult, ALU.add))
                pl.op("ve", lambda e, st=st, lt=lt: e.scalar_tensor_tensor(
                    st[:, 512:768], scr_silu0[:, 0:256], ncols_[:, lt:lt + 1],
                    st[:, 512:768], ALU.mult, ALU.add))
                v = pl.op("ve", lambda e, st=st, lt=lt: e.scalar_tensor_tensor(
                    st[:, 768:1024], scr_silu0[:, 256:512], ccols[:, lt:lt + 1],
                    st[:, 768:1024], ALU.mult, ALU.add))
                pl.wait("mm", "ve", v)
        for lt in range(2):
            tile = egt * 2 + lt
            pl.op("ve", lambda e, tile=tile: e.tensor_scalar(
                scr_pp, RESID[:, tile * D:tile * D + D],
                rcol[:, tile:tile + 1], mrcol[:, tile:tile + 1],
                ALU.mult, ALU.subtract))
            pl.op("ve", lambda e, tile=tile: e.tensor_tensor(
                RESID[:, tile * D:tile * D + D],
                RESID[:, tile * D:tile * D + D], scr_pp, ALU.add))
            pl.op("ve", lambda e, tile=tile, lt=lt: e.tensor_tensor(
                RESID[:, tile * D:tile * D + D],
                RESID[:, tile * D:tile * D + D],
                sreg[:, lt * D:lt * D + D], ALU.add))

    for egt in range(8):
        ffn1_eighth(egt)
    pl.barrier()

    if stage == "ffn1":
        return _finish(nc, pl, RESID, yout)

    pl.wait("gp", "cc", ccN)
    pl.dma("gp", qkn[0:1, 32:48], cc_n_out[0:1, :])
    gv = pl.dma("gp", qkn[0:1, 48:64], cc_n_out[1:2, :])
    pl.wait("ve", "gp", gv)
    pl.op("ve", lambda e: e.tensor_tensor(
        qkn[0:1, 32:48], qkn[0:1, 32:48], qkn[0:1, 48:64], ALU.max))
    v = pl.op("ve", lambda e: e.tensor_tensor(
        qkn[0:1, 32:48], qkn[0:1, 32:48], qrow, ALU.mult))
    pl.wait("ac", "ve", v)
    av = pl.op("ac", lambda e: e.activation(qkn[0:1, 32:48], qkn[0:1, 32:48],
                                            ACT.Sqrt))
    pl.wait("ve", "ac", av)
    pl.op("ve", lambda e: e.tensor_scalar(
        qkn[0:1, 32:48], qkn[0:1, 32:48], -1.01 / 32.0, None, ALU.mult))
    v = pl.op("ve", lambda e: e.tensor_copy(negmb[:], qkn[0:1, 32:48]))
    pl.wait("mm", "ve", v)
    mv = pl.op("mm", _mmf(PS[0][:, 0:16], crow[:], negmb[:], True, True))
    pl.wait("ve", "mm", mv)
    v = pl.op("ve", lambda e: e.tensor_copy(negm[:], PS[0][:, 0:16]))
    pl.wait("ac", "ve", v)
    pl.wait("gp", "cc", pl.c("cc"))
    pl.barrier()

    VSL = (4096, 8320)
    E0B, E1B = 12544, 13568
    _ocs = [VSL[s_] + kt * 132 + off
            for s_ in (0, 1) for kt in range(32) for off in (64, 130)]
    for c0 in _ocs[:-1]:
        pl.opn("ve", lambda e, c0=c0: e.tensor_copy(XN[:, c0:c0 + 2],
                                                    ones2[:]))
    c0 = _ocs[-1]
    msv = pl.op("ve", lambda e, c0=c0: e.tensor_copy(XN[:, c0:c0 + 2],
                                                     ones2[:]))

    _expd, _pvd, _hpdone = {}, {}, {}
    _kv, _vv = {}, {}
    _evacA, _tailv, _zbfv, _expn, _normv = {}, {}, {}, {}, {}
    zhiAB = scr_sq[0:1, 0:1024]
    zloAB = scr_cast[0:1, 0:1024]
    zhiA, zhiB = scr_sq[0:1, 0:512], scr_sq[0:1, 512:1024]
    zloA, zloB = scr_cast[0:1, 0:512], scr_cast[0:1, 512:1024]
    zrowAB = SCRF[0:1, 0:1024]
    zinvAB = SCRF[0:1, 1024:2048]
    LSL = 2048

    def issue_V(hp):
        pl.wait("gp", "mm", _hpdone.get(hp - 2, 0))
        if hp < 2:
            pl.wait("gp", "ve", msv)
        vsl = VSL[hp % 2]
        v = 0
        for kt in range(32):
            pl.dma("gp", XN[:, vsl + kt * 132:vsl + kt * 132 + 64],
                   ccv_out[kt * 128:kt * 128 + 128,
                            hp * 128:hp * 128 + 64])
            v = pl.dma("gp", XN[:, vsl + kt * 132 + 66:vsl + kt * 132 + 130],
                       ccv_out[kt * 128:kt * 128 + 128,
                                hp * 128 + 64:hp * 128 + 128])
        _vv[hp] = v

    def issue_K(hp):
        pl.wait("sy", "mm", _hpdone.get(hp - 1, 0))
        v0 = pl.dma("sy", XN[:, 0:2048], cc_k_out[hp * 128:hp * 128 + 128, :])
        v1 = pl.dma("sy", XN[:, 2048:4096],
                    cc_k_out[D + hp * 128:D + hp * 128 + 128, :])
        _kv[hp] = (v0, v1)

    def eslot(k):
        e0 = (scr_sq[:, (k % 2) * 512:(k % 2) * 512 + 512] if k % 4 < 2 else
              XN[:, E0B + (k % 2) * 512:E0B + (k % 2) * 512 + 512])
        e1 = (scr_cast[:, (k % 2) * 512:(k % 2) * 512 + 512] if k % 4 < 2 else
              XN[:, E1B + (k % 2) * 512:E1B + (k % 2) * 512 + 512])
        return e0, e1

    def emit_expansion(qci):
        B = PS[6 + (qci % 2)]
        pl.wait("mm", "ve", _zbfv[qci])
        if qci - 2 in _normv:
            pl.wait("mm", "ve", _normv[qci - 2])
        pl.opn("mm", _mmf(B, sel2sb[0:1, :], zhiA, True, False))
        pl.opn("mm", _mmf(B, selb[:], zhiB, False, False))
        pl.opn("mm", _mmf(B, sel2sb[0:1, :], zloA, False, False))
        ev = pl.op("mm", _mmf(B, selb[:], zloB, False, True))
        _expn[qci] = ev
        s = LSL + (qci % 2) * 512
        ucol = (qci // 4) * 2048 + (qci % 4) * 512
        pl.wait("ve", "mm", ev)
        pl.op("ve", lambda e, B=B, s=s, ucol=ucol: e.tensor_tensor(
            RY[0:64, ucol:ucol + 512], LAND[0:64, s:s + 512], B[0:64, :],
            ALU.mult))
        v = pl.op("ve", lambda e, B=B, s=s, ucol=ucol: e.tensor_tensor(
            RY[64:128, ucol:ucol + 512], LAND[64:128, s:s + 512],
            B[64:128, :], ALU.mult))
        _normv[qci] = v

    issue_V(0)
    issue_V(1)
    issue_K(0)

    wo_ready = 0
    for cci in range(8):
        pl.wait("gp", "ve")
        lv = pl.dma("gp", LAND[:, (cci % 2) * 1024:(cci % 2) * 1024 + 1024],
                    DI["w1o"][cci, :, :])
        pl.wait("ve", "gp", lv)
        wo_ready = pl.op("ve", lambda e, cci=cci: e.tensor_copy(
            SLAB[:, cci * 1024:cci * 1024 + 1024],
            LAND[:, (cci % 2) * 1024:(cci % 2) * 1024 + 1024]))

    for hp in range(8):
        vsl = VSL[hp % 2]
        pl.wait("mm", "sy", _kv[hp][0])
        pl.wait("mm", "gp", _vv[hp])
        for qc in range(4):
            qci = hp * 4 + qc
            qs0 = RX[0:64, hp * 2048 + qc * 512:hp * 2048 + qc * 512 + 512]
            qs1 = RX[64:128, hp * 2048 + qc * 512:hp * 2048 + qc * 512 + 512]

            def scores(k):
                sb0 = PS[(k % 2) * 2]
                sb1 = PS[(k % 2) * 2 + 1]
                ks0 = XN[0:64, k * 128:k * 128 + 128]
                ks1 = XN[64:128, k * 128:k * 128 + 128]
                pl.opn("mm", _mmf(sb0, ks0, qs0, True, True))
                return pl.op("mm", _mmf(sb1, ks1, qs1, True, True,
                                        tp=(64, 0))), sb0, sb1

            def expk(k, smv, sb0, sb1, extra=0):
                e0, e1 = eslot(k)
                pl.wait("ac", "mm", smv)
                if extra:
                    pl.wait("ac", "mm", extra)
                if k >= 4:
                    pl.wait("ac", "mm", _pvd[k - 4])
                pl.opn("ac", lambda e, sb0=sb0, e0=e0, hp=hp: e.activation(
                    e0, sb0[:], ACT.Exp, scale=1.0 / 32.0,
                    bias=negm[:, 2 * hp:2 * hp + 1]))
                ev = pl.op("ac", lambda e, sb1=sb1, e1=e1, hp=hp:
                           e.activation(
                               e1, sb1[:], ACT.Exp, scale=1.0 / 32.0,
                               bias=negm[:, 2 * hp + 1:2 * hp + 2]))
                _expd[k] = ev

            e31p = _expd.get(31, 0)
            if e31p:
                pl.wait("mm", "ac", e31p)
            smv0, sb00, sb01 = scores(0)
            smv1, sb10, sb11 = scores(1)
            if qci > 0:
                emit_expansion(qci - 1)
            expk(0, smv0, sb00, sb01, extra=_expn.get(qci - 1, 0))
            expk(1, smv1, sb10, sb11, extra=_expn.get(qci - 1, 0))

            pvv = None
            for step in range(2, 34):
                k = step
                if k < 32:
                    if qc == 0 and k == 16:
                        pl.wait("mm", "sy", _kv[hp][1])
                    pl.wait("mm", "ac", _expd[k - 2])
                    smv, sb0, sb1 = scores(k)
                pk = step - 2
                if pk == 0 and qci > 0:
                    pl.wait("mm", "ac", _evacA[qci - 1])
                    pl.wait("mm", "ve", _tailv[qci - 1])
                pl.wait("mm", "ac", _expd[pk])
                p0, p1 = eslot(pk)
                pl.opn("mm", _mmf(PS[4][0:65, :],
                                  XN[:, vsl + pk * 132:vsl + pk * 132 + 65],
                                  p0, pk == 0, pk == 31))
                pvv = pl.op("mm", _mmf(
                    PS[5][0:65, :],
                    XN[:, vsl + pk * 132 + 66:vsl + pk * 132 + 131],
                    p1, pk == 0, pk == 31))
                _pvd[pk] = pvv
                if k < 32:
                    expk(k, smv, sb0, sb1)

            s = LSL + (qci % 2) * 512
            pl.wait("ac", "mm", pvv)
            if qci - 2 in _normv:
                pl.wait("ac", "ve", _normv[qci - 2])
            ev = pl.op("ac", lambda e, s=s: e.activation(
                LAND[0:64, s:s + 512], PS[4][0:64, :], ACT.Identity))
            _evacA[qci] = ev
            pl.wait("ve", "mm", pvv)
            pl.op("ve", lambda e, s=s: e.tensor_copy(
                LAND[64:128, s:s + 512], PS[5][0:64, :]))
            pl.op("ve", lambda e: e.tensor_copy(zrowA, PS[4][64:65, :]))
            tv = pl.op("ve", lambda e: e.tensor_copy(zrowB, PS[5][64:65, :]))
            _tailv[qci] = tv
            pl.op("ve", lambda e: e.reciprocal(zinvAB, zrowAB))
            pl.op("ve", lambda e: e.tensor_copy(zhiAB, zinvAB))
            pl.op("ve", lambda e: e.tensor_tensor(
                zrowAB, zinvAB, zhiAB, ALU.subtract))
            zv = pl.op("ve", lambda e: e.tensor_copy(zloAB, zrowAB))
            _zbfv[qci] = zv
        _hpdone[hp] = pl.c("mm")
        if hp + 2 < 8:
            issue_V(hp + 2)
        if hp + 1 < 8:
            issue_K(hp + 1)
    emit_expansion(31)
    pl.barrier()

    bo_v = pl.dma("gp", scr_pp, DI["boex"])
    pl.wait("ve", "gp", bo_v)
    pl.wait("mm", "ve", wo_ready)
    _opv = {}
    for qt in range(NT):
        for eh in range(2):
            key = qt * 2 + eh
            bank = PS[key % 4]
            if key >= 4:
                pl.wait("mm", "ve", _opv[key - 4])
            mv = None
            for cci in range(8):
                f = _mmf(bank, RY[:, cci * 2048 + qt * 128:
                                  cci * 2048 + qt * 128 + 128],
                         SLAB[:, cci * 1024 + eh * 512:
                              cci * 1024 + eh * 512 + 512],
                         cci == 0, cci == 7)
                if cci == 7:
                    mv = pl.op("mm", f)
                else:
                    pl.opn("mm", f)
            pl.wait("ve", "mm", mv)
            pl.op("ve", lambda e, bank=bank, eh=eh: e.tensor_tensor(
                scr_silu0, bank[:], scr_pp[:, eh * 512:eh * 512 + 512],
                ALU.add))
            v = pl.op("ve", lambda e, qt=qt, eh=eh: e.tensor_tensor(
                RESID[:, qt * D + eh * 512:qt * D + eh * 512 + 512],
                RESID[:, qt * D + eh * 512:qt * D + eh * 512 + 512],
                scr_silu0, ALU.add))
            _opv[key] = v
    pl.barrier()

    return _finish(nc, pl, RESID, yout)


def _finish(nc, pl, RESID, yout):
    pl.barrier()
    last = 0
    for tile in range(NT):
        last = pl.dma("sy", yout[tile * 128:(tile + 1) * 128, :],
                      RESID[:, tile * D:(tile + 1) * D])
    pl.wait("sy", "sy", last)
    pl.wait("gp", "cc", pl.c("cc"))
    _emit(nc, pl)
    return nc


def _emit(nc, pl):
    import contextlib
    with contextlib.ExitStack() as stack:
        S = {s: stack.enter_context(nc.semaphore(f"sem_{s}")) for s in SEMS}
        with nc.Block() as block:
            @block.sync
            def _(eng):
                for f in pl.q["sy"]:
                    f(eng, S)

            @block.gpsimd
            def _(eng):
                for f in pl.q["gp"]:
                    f(eng, S)

            @block.scalar
            def _(eng):
                for f in pl.q["ac"]:
                    f(eng, S)

            @block.vector
            def _(eng):
                for f in pl.q["ve"]:
                    f(eng, S)

            @block.tensor
            def _(eng):
                for f in pl.q["mm"]:
                    f(eng, S)



def kernel(**inputs):
    stage = DEBUG_STAGE or "full"
    nc = build_nc(stage)
    in_maps = make_in_maps(inputs)
    res = run_bass_kernel_spmd(nc, in_maps, core_ids=list(range(NCORE)))
    out = np.empty((4, SEQ, D), dtype=np.float32)
    for c in range(NCORE):
        b, h = c // 2, c % 2
        out[b, h * T:(h + 1) * T, :] = res.results[c]["y"]
    return out
